# version 03
# baseline (speedup 1.0000x reference)
import math
import numpy as np
import ml_dtypes
import concourse.bass as bass
import concourse.mybir as mybir
from concourse.bass_utils import run_bass_kernel_spmd

F32 = mybir.dt.float32
BF16 = mybir.dt.bfloat16
AF = mybir.ActivationFunctionType
ALU = mybir.AluOpType
AX = mybir.AxisListType
BF = ml_dtypes.bfloat16

NH = 32
DM = 4096
SEQ = 4096
OWN = 2048
NEPS = 1e-5
SCALE = 1.0 / math.sqrt(192.0)
VSTRIDE = 132
KB = 1024



class Buf:
    __slots__ = ("name", "writers", "readers")

    def __init__(self, name):
        self.name = name
        self.writers = {}
        self.readers = []


class Op:
    __slots__ = ("eng", "fn", "deps", "dma", "used", "val", "tag")

    def __init__(self, eng, fn, dma, tag):
        self.eng, self.fn, self.dma, self.tag = eng, fn, dma, tag
        self.deps = []
        self.used = False
        self.val = None


def sem_of(o):
    return ("d" + o.eng) if o.dma else o.eng


class G:
    def __init__(self):
        self.ops = []
        self.bufs = []
        self.last = {}

    def newbuf(self, name):
        b = Buf(name)
        self.bufs.append(b)
        return b

    def op(self, eng, fn, reads=(), writes=(), deps=(), dma=False, tag=""):
        o = Op(eng, fn, dma, tag)
        d = []
        for b in reads:
            d.extend(b.writers.values())
        for b in writes:
            d.extend(b.writers.values())
            d.extend(b.readers)
        for b in reads:
            b.readers.append(o)
        for b in writes:
            b.writers = dict(b.writers)
            b.writers[sem_of(o)] = o
            b.readers = []
        d.extend(x for x in deps if x is not None)
        seen = set()
        for x in d:
            if id(x) not in seen and x is not o:
                seen.add(id(x))
                o.deps.append(x)
        self.ops.append(o)
        self.last[sem_of(o)] = o
        return o

    def barrier(self):
        lasts = [self.last[k] for k in list(self.last)]
        bops = {}
        for eng in ("T", "V", "A", "P", "S"):
            bops[eng] = self.op(eng, lambda e: e.nop(), deps=lasts, tag=f"bar_{eng}")
        all_b = list(bops.values())
        for eng in ("T", "V", "A", "P", "S"):
            self.op(eng, lambda e: e.nop(), deps=all_b, tag=f"bar2_{eng}")
        for b in self.bufs:
            b.writers = {"T": self.last["T"]}
            b.readers = []

    @staticmethod
    def _skip(d, o):
        if d.eng == "T" and o.eng == "T" and not d.dma and not o.dma:
            return True
        if d.dma and o.dma and d.eng == o.eng:
            return True
        return False

    def emit_streams(self, sems):
        for o in self.ops:
            for d in o.deps:
                if not self._skip(d, o):
                    d.used = True
        counters = {}
        for o in self.ops:
            if o.used or o.dma:
                s = sem_of(o)
                counters[s] = counters.get(s, 0) + (16 if o.dma else 1)
                o.val = counters[s]
        waited = {}

        def run(eng_key, eng_obj):
            for o in self.ops:
                if o.eng != eng_key:
                    continue
                for d in o.deps:
                    if self._skip(d, o):
                        continue
                    s = sem_of(d)
                    v = d.val
                    assert v is not None, (d.tag, o.tag)
                    if waited.get((eng_key, s), 0) >= v:
                        continue
                    eng_obj.wait_ge(sems[s], v)
                    waited[(eng_key, s)] = v
                inst = o.fn(eng_obj)
                if o.used or o.dma:
                    assert inst is not None, o.tag
                    inst.then_inc(sems[sem_of(o)], 16 if o.dma else 1)

        return run




def host_layout(inputs):
    d = {}
    Wq_c = np.asarray(inputs["L0_attn_Wq_c"])
    Wq_up = np.asarray(inputs["L0_attn_Wq_up"])
    Wkv_c = np.asarray(inputs["L0_attn_Wkv_c"])
    Wkv_up = np.asarray(inputs["L0_attn_Wkv_up"])
    Wo = np.asarray(inputs["L0_attn_Wo"])
    W = np.asarray(inputs["L0_ffn_W"])
    A = np.asarray(inputs["L0_ffn_A"])
    Bf = np.asarray(inputs["L0_ffn_Bf"])

    d["Wqc_t"] = np.ascontiguousarray(
        Wq_c.reshape(12, 128, 32, 128).transpose(0, 3, 2, 1))
    d["Wkvc_t"] = np.ascontiguousarray(
        Wkv_c[:512].reshape(4, 128, 32, 128).transpose(0, 3, 2, 1))
    d["Wkvc_r"] = np.ascontiguousarray(
        Wkv_c[512:].reshape(64, 32, 128).transpose(2, 1, 0))

    Wqu = Wq_up.reshape(16, 2, 192, 1536)
    per_pair = np.concatenate(
        [Wqu[:, 0, :128], Wqu[:, 1, :128], Wqu[:, 0, 128:], Wqu[:, 1, 128:]], axis=1)
    d["Wqup_t"] = np.ascontiguousarray(
        per_pair.reshape(16, 384, 12, 128).transpose(0, 3, 2, 1))
    bqu = np.asarray(inputs["L0_attn_bq_up"]).reshape(16, 2, 192)
    bqu_r = np.concatenate(
        [bqu[:, 0, :128], bqu[:, 1, :128], bqu[:, 0, 128:], bqu[:, 1, 128:]], axis=1)
    d["bqup_t"] = np.ascontiguousarray(
        bqu_r.reshape(16, 3, 128).transpose(2, 0, 1).reshape(128, 48))

    Wku = Wkv_up.reshape(32, 2, 128, 512)
    d["Wkup_t"] = np.ascontiguousarray(
        Wku[:, 0].reshape(32, 128, 4, 128).transpose(0, 3, 2, 1))
    d["Wvup_t"] = np.ascontiguousarray(
        Wku[:, 1].reshape(32, 128, 4, 128).transpose(0, 3, 2, 1))
    bku = np.asarray(inputs["L0_attn_bkv_up"]).reshape(32, 2, 128)
    d["bkup_t"] = np.ascontiguousarray(bku[:, 0].T)
    d["bvupT"] = np.ascontiguousarray(bku[:, 1].T)

    d["WoT"] = np.ascontiguousarray(Wo.T)
    d["WffnT"] = np.ascontiguousarray(W.T)
    d["WffnT_t"] = np.ascontiguousarray(
        W.T.reshape(4, 8, 128, 16, 256).transpose(3, 0, 2, 1, 4))
    Afl = A.reshape(256, 4096)
    d["A_t"] = np.ascontiguousarray(
        Afl.reshape(2, 128, 32, 128).transpose(2, 3, 0, 1))
    d["Bf_t"] = np.ascontiguousarray(
        Bf.transpose(0, 2, 1).reshape(256, 4096).reshape(2, 128, 4096))

    d["bqc_t"] = np.ascontiguousarray(
        np.asarray(inputs["L0_attn_bq_c"]).reshape(12, 128).T)
    d["bkvc_t"] = np.ascontiguousarray(
        np.pad(np.asarray(inputs["L0_attn_bkv_c"]), (0, 64)).reshape(5, 128).T)
    d["bo_row"] = np.ascontiguousarray(
        np.asarray(inputs["L0_attn_bo"]).reshape(1, 4096))
    d["fb_row"] = np.ascontiguousarray(
        np.asarray(inputs["L0_ffn_bias"]).reshape(1, 4096))
    d["bo_tile"] = np.ascontiguousarray(
        np.tile(np.asarray(inputs["L0_attn_bo"]).reshape(1, 4096), (128, 1)))
    d["fb_tile"] = np.ascontiguousarray(
        np.tile(np.asarray(inputs["L0_ffn_bias"]).reshape(1, 4096), (128, 1)))
    d["coeffs"] = np.ascontiguousarray(np.asarray(inputs["L0_ffn_coeffs"]).reshape(16, 1))
    d["scales"] = np.ascontiguousarray(np.asarray(inputs["L0_ffn_scales"]).reshape(16, 1))

    d["ident"] = np.eye(128, dtype=BF)
    d["ones1"] = np.ones((1, 128), dtype=BF)
    d["cvec"] = np.tile(np.array([[1e-5, -0.5, 1.0 / math.sqrt(192.0),
                                   1.0 / 4096.0, 0.0, 1.0]],
                                 dtype=np.float32), (128, 1))
    sel = np.zeros((16, 256), dtype=np.float32)
    for col in range(256):
        sel[col // 16, col] = 1.0
    d["sel"] = sel
    return d




DBG_SKIP = set()


def build_program(tap=None):
    nc = bass.Bass("TRN2", target_bir_lowering=False)
    g = G()

    xkv = nc.dram_tensor("xkv", [SEQ, DM], F32, kind="ExternalInput").ap()
    dr = {}
    for name, shape, dt in [
        ("Wqc_t", [12, 128, 32, 128], F32), ("Wkvc_t", [4, 128, 32, 128], F32),
        ("Wkvc_r", [128, 32, 64], F32), ("Wqup_t", [16, 128, 12, 384], F32),
        ("Wkup_t", [32, 128, 4, 128], F32), ("Wvup_t", [32, 128, 4, 128], F32),
        ("bkup_t", [128, 32], F32), ("bvupT", [128, 32], F32),
        ("bqup_t", [128, 48], F32),
        ("WoT", [128, 4096], F32), ("WffnT", [4096, 4096], F32),
        ("WffnT_t", [16, 4, 128, 8 * 256], F32),
        ("A_t", [32, 128, 2, 128], F32), ("Bf_t", [2, 128, 4096], F32),
        ("bqc_t", [128, 12], F32), ("bkvc_t", [128, 5], F32),
        ("bo_row", [1, 4096], F32), ("fb_row", [1, 4096], F32),
        ("bo_tile", [128, 4096], F32), ("fb_tile", [128, 4096], F32),
        ("coeffs", [16, 1], F32), ("scales", [16, 1], F32),
        ("ident", [128, 128], BF16), ("ones1", [1, 128], BF16),
        ("cvec", [128, 6], F32),
        ("sel", [16, 256], F32),
    ]:
        dr[name] = nc.dram_tensor(name, shape, dt, kind="ExternalInput").ap()

    out = nc.dram_tensor("out", [OWN, DM], F32, kind="ExternalOutput").ap()
    xnew_hbm = nc.dram_tensor("xnew_scr", [OWN, DM], F32, kind="Internal").ap()
    t_kv = t_q = t_os = t_xn = None
    if tap == "lat":
        t_kv = nc.dram_tensor("t_kv", [5 * 128, SEQ], F32, kind="ExternalOutput").ap()
        t_q = nc.dram_tensor("t_q", [12 * 128, OWN], F32, kind="ExternalOutput").ap()
    if tap == "osum":
        t_os = nc.dram_tensor("t_os", [128, 16 * 128], F32, kind="ExternalOutput").ap()
    if tap == "xnew":
        t_xn = nc.dram_tensor("t_xn", [OWN, DM], F32, kind="ExternalOutput").ap()

    SBUF_BASE = 16384
    SBUF_END = 229247
    cur = [SBUF_BASE + 0]

    def zalloc(name, shape, dt, at=None):
        nbytes = int(np.prod(shape[1:])) * (4 if dt == F32 else 2) if len(shape) > 1 \
            else 4
        if len(shape) == 2:
            nbytes = shape[1] * (4 if dt == F32 else 2)
        if at is None:
            at = (cur[0] + 31) & ~31
            cur[0] = at + nbytes
        t = nc.alloc_sbuf_tensor_at(name, shape, dt, offset=at)
        return t, at, at + nbytes

    ident, _, _ = zalloc("ident", [128, 128], BF16)
    ones1, _, _ = zalloc("ones1", [1, 128], BF16)
    bqc, _, _ = zalloc("bqc", [128, 12], F32)
    bkvc, _, _ = zalloc("bkvc", [128, 5], F32)
    bqup, _, _ = zalloc("bqup", [128, 48], F32)
    bkup, _, _ = zalloc("bkup", [128, 32], F32)
    bvupT, _, _ = zalloc("bvupT", [128, 32], F32)
    selw, _, _ = zalloc("selw", [16, 256], F32)
    cfs, _, _ = zalloc("cfs", [16, 2], F32)
    w16, _, _ = zalloc("w16", [16, 1], F32)
    wvec, _, _ = zalloc("wvec", [128, 2], F32)
    cur[0] = (cur[0] + 255) & ~255
    red, _, _ = zalloc("red", [128, 12], F32)
    cur[0] = (cur[0] + 255) & ~255
    ssq, _, _ = zalloc("ssq", [128, 8], F32)
    rcp, _, _ = zalloc("rcp", [128, 4], F32)
    Bv, _, _ = zalloc("Bv", [128, 1], F32)
    Bvb, _, _ = zalloc("Bvb", [128, 1], BF16)
    constv, _, _ = zalloc("constv", [128, 6], F32)
    Z0_END = (cur[0] + 31) & ~31
    SB_CAP = nc.sbuf_bytes_remaining + 0
    BASE = Z0_END

    def at(off, name, shape, dt, end_limit=SBUF_END):
        t, a, e = zalloc(name, shape, dt, at=off)
        assert e <= end_limit, (name, e)
        return t

    kvlat = [at(BASE + i * 8 * KB, f"kvlat{i}", [128, SEQ], BF16) for i in range(4)]
    kvrope = at(BASE + 32 * KB, "kvrope", [128, SEQ], BF16)
    QBASE = BASE + 40 * KB
    qlat = [at(QBASE + m * 4 * KB, f"qlat{m}", [128, OWN], BF16) for m in range(12)]
    R3 = QBASE + 48 * KB

    lnfm = at(R3, "lnfm", [128, 32 * 512], BF16)
    xt = [at(R3 + 32 * KB, "xt0", [128, DM], F32),
          at(R3 + 48 * KB, "xt1", [128, DM], F32)]
    wbf = [at(R3 + 48 * KB, "wbf0", [128, 4096], BF16),
           at(R3 + 56 * KB, "wbf1", [128, 4096], BF16)]
    lntm = at(R3 + 64 * KB, "lntm", [128, DM], BF16)
    wstg32 = [at(R3 + 72 * KB, "wstg32_0", [128, 4096], F32),
              at(R3 + 88 * KB, "wstg32_1", [128, 4096], F32)]
    kvcb = at(QBASE, "kvcb", [128, 32 * 576], BF16)
    CW = 576

    KT = [at(R3, "KT0", [128, SEQ], BF16), at(R3 + 8 * KB, "KT1", [128, SEQ], BF16)]
    Vt = [at(R3 + 16 * KB, "Vt0", [128, 32 * VSTRIDE], BF16),
          at(R3 + 25 * KB, "Vt1", [128, 32 * VSTRIDE], BF16)]
    qt = [[at(R3 + 34 * KB + (i * 3 + m) * 4 * KB, f"qt{i}_{m}", [128, OWN], BF16)
           for m in range(3)] for i in range(2)]
    pt = [at(R3 + 58 * KB, "pt0", [128, 512], BF16),
          at(R3 + 59 * KB, "pt1", [128, 512], BF16)]
    kvw32 = [at(R3 + 60 * KB, "kvw32_0", [128, 1024], F32),
             at(R3 + 64 * KB, "kvw32_1", [128, 1024], F32)]
    kvwb = [at(R3 + 68 * KB, "kvwb0", [128, 1024], BF16),
            at(R3 + 70 * KB, "kvwb1", [128, 1024], BF16)]
    quw32 = at(R3 + 72 * KB, "quw32", [128, 12 * 384], F32)
    quwb = at(R3 + 90 * KB, "quwb", [128, 12 * 384], BF16)
    osum = at(R3 + 99 * KB, "osum", [128, 16 * 128], F32)

    lnfm2 = at(BASE, "lnfm2", [128, 32 * 1024], BF16)
    P3B = BASE + 64 * KB
    xn = at(P3B, "xn", [128, DM], F32)
    xt3 = at(P3B + 16 * KB, "xt3", [128, DM], F32)
    lntm2 = at(P3B + 32 * KB, "lntm2", [128, DM], BF16)
    osT = at(P3B + 40 * KB, "osT", [128, 16 * 128], BF16)
    WoTb = at(P3B + 44 * KB, "WoTb", [128, 4096], BF16)
    bobc = at(P3B + 52 * KB, "bobc", [128, 4096], BF16)
    fbbc = at(P3B + 60 * KB, "fbbc", [128, 4096], BF16)
    xa = at(P3B + 68 * KB, "xa", [128, 2 * 1024], BF16)
    BfTb = at(P3B + 72 * KB, "BfTb", [128, 2 * 4096], BF16)
    wpan = [at(P3B + 88 * KB, "wpan0", [128, 32 * 256], BF16),
            at(P3B + 104 * KB, "wpan1", [128, 32 * 256], BF16)]
    ATb = at(P3B + 88 * KB, "ATb", [128, 32 * 256], BF16)
    borow = at(P3B + 88 * KB, "borow", [1, 4096], F32)
    brow2 = at(P3B + 104 * KB, "brow2", [1, 4096], F32)
    xstrip = at(P3B + 122 * KB, "xstrip", [128, 8 * 256], F32)
    ostrip = at(P3B + 130 * KB, "ostrip", [128, 8 * 256], F32)
    wostg = at(P3B + 122 * KB, "wostg", [128, 4096], F32)

    ps = [nc.alloc_psum_tensor(f"ps{i}", [128, 512], F32) for i in range(6)]
    psb = [nc.alloc_psum_tensor(f"psb{i}", [128, 1024], BF16) for i in range(2)]

    bufs = {}

    def buf(name):
        if name not in bufs:
            bufs[name] = g.newbuf(name)
        return bufs[name]

    psB = [buf(f"ps{i}") for i in range(6)]
    psbB = [buf(f"psb{i}") for i in range(2)]
    ZW01 = buf("ZW01")

    def dma(engq, dst_ap, src_ap, reads=(), writes=(), deps=(), tag=""):
        return g.op(engq, lambda e, d=dst_ap, s=src_ap: e.dma_start(d, s),
                    reads=reads, writes=writes, deps=deps, dma=True, tag=tag)

    _cl = [("ident", ident), ("ones1", ones1), ("bqc_t", bqc),
           ("bkvc_t", bkvc), ("bqup_t", bqup), ("bkup_t", bkup),
           ("bvupT", bvupT), ("sel", selw), ("cvec", constv)]
    if "consts" in DBG_SKIP:
        _cl = [("ident", ident), ("cvec", constv)]
    for name, dst in _cl:
        dma("S", dst[:, :] if name != "ones1" else dst[0:1, :], dr[name],
            writes=[buf(name)], tag=f"ld_{name}")
    dma("S", cfs[:, 0:1], dr["coeffs"], writes=[buf("cfs")], tag="ld_cf")
    dma("S", cfs[:, 1:2], dr["scales"], writes=[buf("cfs")], tag="ld_sc")
    for wi in range(4):
        g.op("A", lambda e, wi=wi: e.activation(red[:, 9:10], constv[:, 0:1],
                                                [AF.Identity, AF.Exp, AF.Ln,
                                                 AF.Identity][wi],
                                                bias=constv[:, 4:5],
                                                scale=constv[:, 5:6]),
             reads=[buf("cvec")], writes=[buf("warm")], tag=f"warm{wi}")
    g.op("V", lambda e: e.tensor_tensor(w16[:, :], cfs[:, 0:1], cfs[:, 1:2], ALU.mult),
         reads=[buf("cfs")], writes=[buf("w16")], tag="w16")
    if "wvec" not in DBG_SKIP:
        for m in range(2):
            g.op("T", lambda e, m=m: e.matmul(ps[5][:, m:m + 1],
                                              selw[:, m * 128:(m + 1) * 128],
                                              w16[:, :], start=True, stop=True),
                 reads=[buf("sel"), buf("w16")], writes=[psB[5]], tag=f"wvm{m}")
        sp = g.op("T", lambda e: e.matmul(ps[4][:, 0:1], selw[:, 0:128],
                                          w16[:, :], start=True, stop=True),
                  reads=[buf("sel"), buf("w16")], writes=[psB[4]], tag="wvsp")
        for m in range(2):
            g.op("A", lambda e, m=m: e.activation(wvec[:, m:m + 1], ps[5][:, m:m + 1],
                                                  AF.Identity, bias=constv[:, 4:5],
                                                  scale=constv[:, 5:6]),
                 reads=[psB[5], buf("cvec")], writes=[buf("wvec")], deps=[sp],
                 tag=f"wve{m}")

    def ln_tile(x_ap, xb, rstd_dst, dstb, tag):
        g.op("V", lambda e: e.tensor_reduce(red[:, 0:1], x_ap, AX.X, ALU.add,
                                            negate=True),
             reads=[xb], writes=[buf("red")], tag=f"{tag}rs")
        g.op("V", lambda e: e.scalar_tensor_tensor(rstd_dst[:, :], x_ap, 1.0, x_ap,
                                                   ALU.mult, ALU.mult,
                                                   accum_out=ssq[:, 0:1]),
             reads=[xb], writes=[buf("red"), dstb], tag=f"{tag}sq")
        g.op("V", lambda e: e.tensor_scalar(red[:, 2:3], red[:, 0:1], 1.0 / DM, None,
                                            ALU.mult),
             reads=[buf("red")], writes=[buf("red")], tag=f"{tag}nm")
        g.op("V", lambda e: e.tensor_tensor(red[:, 3:4], red[:, 2:3], red[:, 2:3],
                                            ALU.mult),
             reads=[buf("red")], writes=[buf("red")], tag=f"{tag}m2")
        g.op("V", lambda e: e.tensor_scalar(red[:, 4:5], red[:, 3:4], -1.0, NEPS,
                                            ALU.mult, ALU.add),
             reads=[buf("red")], writes=[buf("red")], tag=f"{tag}m2e")
        g.op("V", lambda e: e.scalar_tensor_tensor(red[:, 5:6], ssq[:, 0:1], 1.0 / DM,
                                                   red[:, 4:5], ALU.mult, ALU.add),
             reads=[buf("red")], writes=[buf("red")], tag=f"{tag}v")
        g.op("A", lambda e: e.activation(red[:, 6:7], red[:, 5:6], AF.Ln,
                                         bias=constv[:, 4:5], scale=constv[:, 5:6]),
             reads=[buf("red"), buf("cvec")], writes=[buf("red")], tag=f"{tag}l")
        g.op("A", lambda e: e.activation(red[:, 7:8], red[:, 6:7], AF.Exp,
                                         bias=constv[:, 4:5], scale=constv[:, 1:2]),
             reads=[buf("red"), buf("cvec")], writes=[buf("red")], tag=f"{tag}r")
        g.op("V", lambda e: e.tensor_scalar(rstd_dst[:, :], x_ap, red[:, 2:3],
                                            red[:, 7:8], ALU.add, ALU.mult),
             reads=[xb, buf("red")], writes=[dstb], tag=f"{tag}n")

    def transposes(src, srcb, tt, dstfm, dst_buf, wtok, tag):
        for cg in range(4):
            pb = psbB[cg % 2]
            def tr(e, cg=cg):
                r = None
                for k in range(8):
                    c = cg * 8 + k
                    r = e.transpose(psb[cg % 2][:, k * 128:(k + 1) * 128],
                                    src[:, c * 128:(c + 1) * 128], ident[:, :])
                return r
            g.op("T", tr, reads=[srcb, buf("ident")], writes=[pb],
                 tag=f"{tag}t{cg}")
            def ev(e, cg=cg):
                r = None
                for k in range(8):
                    c = cg * 8 + k
                    r = e.tensor_copy(
                        dstfm[:, c * wtok + tt * 128:c * wtok + (tt + 1) * 128],
                        psb[cg % 2][:, k * 128:(k + 1) * 128])
                return r
            g.op("V", ev, reads=[pb], writes=[dst_buf], tag=f"{tag}e{cg}")

    for m in range(4) if "kvc" not in DBG_SKIP else []:
        dw = dma("S", wstg32[m % 2][:, :], dr["Wkvc_t"][m],
                 writes=[buf(f"wstg32_{m % 2}")], tag=f"kvcl{m}")
        g.op("V", lambda e, m=m: e.tensor_copy(
            kvcb[:, :].rearrange("p (c w) -> p c w", c=32)[:, :, m * 128:(m + 1) * 128],
            wstg32[m % 2][:, :].rearrange("p (c w) -> p c w", c=32)),
            reads=[buf(f"wstg32_{m % 2}")], writes=[buf("kvcb")], deps=[dw],
            tag=f"kvcc{m}")
    dw = dma("S", wstg32[0][:, 0:32 * 64], dr["Wkvc_r"], writes=[buf("wstg32_0")],
             tag="kvclr") if "kvc" not in DBG_SKIP else None
    ("kvc" in DBG_SKIP) or g.op("V", lambda e: e.tensor_copy(
        kvcb[:, :].rearrange("p (c w) -> p c w", c=32)[:, :, 512:576],
        wstg32[0][:, 0:32 * 64].rearrange("p (c w) -> p c w", c=32)),
        reads=[buf("wstg32_0")], writes=[buf("kvcb")], deps=[dw], tag="kvccr")

    for grp in range(8):
        for tt in range(4):
            tok0 = grp * 512 + tt * 128
            xb = buf(f"xt{tt % 2}")
            dx = dma("S", xt[tt % 2][:, :], xkv[tok0:tok0 + 128, :], writes=[xb],
                     tag=f"Ax{grp}_{tt}")
            ln_tile(xt[tt % 2][:, :], xb, lntm, buf("lntm"), f"A{grp}_{tt}")
            if "tr" not in DBG_SKIP:
                transposes(lntm, buf("lntm"), tt, lnfm, buf("lnfm"), 512,
                           f"At{grp}_{tt}")
            if tap == "ln" and grp == 0 and tt == 0:
                t_ln = nc.dram_tensor("t_ln", [128, DM], BF16,
                                      kind="ExternalOutput").ap()
                t_fm = nc.dram_tensor("t_fm", [128, 32 * 512], BF16,
                                      kind="ExternalOutput").ap()
                t_id = nc.dram_tensor("t_id", [128, 128], BF16,
                                      kind="ExternalOutput").ap()
                t_red = nc.dram_tensor("t_red", [128, 12], F32,
                                       kind="ExternalOutput").ap()
                t_ssq = nc.dram_tensor("t_ssq", [128, 8], F32,
                                       kind="ExternalOutput").ap()
                t_x = nc.dram_tensor("t_x", [128, DM], F32,
                                     kind="ExternalOutput").ap()
                fin = [dma("P", t_ln, lntm[:, :], reads=[buf("lntm")], tag="tln"),
                       dma("P", t_fm, lnfm[:, :], reads=[buf("lnfm")], tag="tfm"),
                       dma("P", t_id, ident[:, :], reads=[buf("ident")], tag="tid"),
                       dma("P", t_red, red[:, :], reads=[buf("red")], tag="tred"),
                       dma("P", t_ssq, ssq[:, :], reads=[buf("red")], tag="tssq"),
                       dma("P", t_x, xt[0][:, :], reads=[buf("xt0")], tag="tx")]
                g.op("P", lambda e: e.nop(), deps=fin, tag="fin")
                return nc, g
        for m in range(5):
            mw = 128 if m < 4 else 64
            pb = psB[m % 2]
            def mmk(e, m=m, mw=mw):
                r = None
                for c in range(32):
                    r = e.matmul(ps[m % 2][0:mw, :],
                                 kvcb[:, c * CW + m * 128:c * CW + m * 128 + mw],
                                 lnfm[:, c * 512:(c + 1) * 512],
                                 start=(c == 0), stop=(c == 31))
                return r
            g.op("T", mmk, reads=[buf("kvcb"), buf("lnfm")], writes=[pb],
                 tag=f"Am{grp}_{m}")
            dst = kvlat[m] if m < 4 else kvrope
            g.op("A", lambda e, m=m, mw=mw, grp=grp, dst=dst: e.activation(
                dst[0:mw, grp * 512:(grp + 1) * 512], ps[m % 2][0:mw, :], AF.Identity,
                bias=bkvc[0:mw, m:m + 1], scale=constv[0:mw, 5:6]),
                reads=[pb, buf("bkvc_t"), buf("cvec")], writes=[buf("kvlat")],
                tag=f"Ae{grp}_{m}")

    dma("S", kvrope[64:128, :], kvrope[0:64, :], reads=[buf("kvlat")],
        writes=[buf("kvlat")], tag="ropedup")

    g.barrier()

    for qr in range(4):
        for tt in range(4):
            tok0 = qr * 512 + tt * 128
            xb = buf("xt0")
            dx = dma("P", xt[0][:, :], xkv[tok0:tok0 + 128, :], writes=[xb],
                     tag=f"Bx{qr}_{tt}")
            ln_tile(xt[0][:, :], xb, lntm, buf("lntm"), f"B{qr}_{tt}")
            transposes(lntm, buf("lntm"), tt, lnfm, buf("lnfm"), 512, f"Bt{qr}_{tt}")
        for m in range(12):
            wsb = buf(f"wstg32_{m % 2}")
            dw = dma("S", wstg32[m % 2][:, :], dr["Wqc_t"][m], writes=[wsb],
                     tag=f"Bw{qr}_{m}")
            cb = buf(f"wbf{m % 2}")
            g.op("V", lambda e, m=m: e.tensor_copy(wbf[m % 2][:, :],
                                                   wstg32[m % 2][:, :]),
                 reads=[wsb], writes=[cb], deps=[dw], tag=f"Bc{qr}_{m}")
            pb = psB[2 + (m % 2)]
            def mmq(e, m=m):
                r = None
                for c in range(32):
                    r = e.matmul(ps[2 + (m % 2)][:, :],
                                 wbf[m % 2][:, c * 128:(c + 1) * 128],
                                 lnfm[:, c * 512:(c + 1) * 512],
                                 start=(c == 0), stop=(c == 31))
                return r
            g.op("T", mmq, reads=[cb, buf("lnfm")], writes=[pb], tag=f"Bm{qr}_{m}")
            g.op("A", lambda e, m=m, qr=qr: e.activation(
                qlat[m][:, qr * 512:(qr + 1) * 512], ps[2 + (m % 2)][:, :], AF.Identity,
                bias=bqc[:, m:m + 1], scale=constv[:, 5:6]),
                reads=[pb, buf("bqc_t"), buf("cvec")], writes=[buf("qlat")],
                tag=f"Be{qr}_{m}")

    if tap == "lat":
        fin = []
        for m in range(4):
            fin.append(dma("P", t_kv[m * 128:(m + 1) * 128, :], kvlat[m][:, :],
                           reads=[buf("kvlat")], tag=f"tkv{m}"))
        fin.append(dma("P", t_kv[4 * 128:4 * 128 + 64, :], kvrope[0:64, :],
                       reads=[buf("kvlat")], tag="tkvr"))
        for m in range(12):
            fin.append(dma("P", t_q[m * 128:(m + 1) * 128, :], qlat[m][:, :],
                           reads=[buf("qlat")], tag=f"tq{m}"))
        g.op("P", lambda e: e.nop(), deps=fin, tag="fin")
        return nc, g

    g.barrier()

    g.op("P", lambda e: e.memset(osum[:, :], 0.0), writes=[buf("osum")], tag="os0")
    for par in range(2):
        g.op("P", lambda e, par=par: e.memset(
            Vt[par][:, :].rearrange("p (k w) -> p k w", k=32)[:, :, 128:129], 1.0),
            writes=[buf(f"Vt{par}")], tag=f"vone{par}")

    def gen_pair(pr):
        par = pr % 2
        dw = dma("S", quw32[:, :], dr["Wqup_t"][pr], writes=[buf("quw32")],
                 tag=f"qw{pr}")
        g.op("V", lambda e: e.tensor_copy(quwb[:, :], quw32[:, :]),
             reads=[buf("quw32")], writes=[buf("quwb")], deps=[dw], tag=f"qwc{pr}")
        for m3 in range(3):
            for qc in range(4):
                pb = psB[4 + ((m3 * 4 + qc) % 2)]
                def mmq(e, m3=m3, qc=qc):
                    r = None
                    for c in range(12):
                        r = e.matmul(ps[4 + ((m3 * 4 + qc) % 2)][:, :],
                                     quwb[:, c * 384 + m3 * 128:c * 384 + (m3 + 1) * 128],
                                     qlat[c][:, qc * 512:(qc + 1) * 512],
                                     start=(c == 0), stop=(c == 11))
                    return r
                g.op("T", mmq, reads=[buf("quwb"), buf("qlat")], writes=[pb],
                     tag=f"qg{pr}_{m3}_{qc}")
                g.op("A", lambda e, m3=m3, qc=qc, pr=pr, par=par: e.activation(
                    qt[par][m3][:, qc * 512:(qc + 1) * 512],
                    ps[4 + ((m3 * 4 + qc) % 2)][:, :], AF.Identity,
                    bias=bqup[:, pr * 3 + m3:pr * 3 + m3 + 1], scale=constv[:, 5:6]),
                    reads=[pb, buf("bqup_t"), buf("cvec")], writes=[buf(f"qt{par}")],
                    tag=f"qe{pr}_{m3}_{qc}")

    def gen_head(h):
        par = h % 2
        dwk = dma("S", kvw32[par][:, 0:512], dr["Wkup_t"][h],
                  writes=[buf(f"kvw32_{par}")], tag=f"kw{h}")
        dwv = dma("S", kvw32[par][:, 512:1024], dr["Wvup_t"][h],
                  writes=[buf(f"kvw32_{par}")], tag=f"vw{h}")
        g.op("V", lambda e, par=par: e.tensor_copy(kvwb[par][:, :], kvw32[par][:, :]),
             reads=[buf(f"kvw32_{par}")], writes=[buf(f"kvwb{par}")],
             deps=[dwk, dwv], tag=f"kvc{h}")
        for cc in range(8):
            pb = psB[4 + (cc % 2)]
            def mmk(e, cc=cc, par=par):
                r = None
                for c in range(4):
                    r = e.matmul(ps[4 + (cc % 2)][:, :],
                                 kvwb[par][:, c * 128:(c + 1) * 128],
                                 kvlat[c][:, cc * 512:(cc + 1) * 512],
                                 start=(c == 0), stop=(c == 3))
                return r
            g.op("T", mmk, reads=[buf(f"kvwb{par}"), buf("kvlat")], writes=[pb],
                 tag=f"kg{h}_{cc}")
            g.op("A", lambda e, cc=cc, par=par, h=h: e.activation(
                KT[par][:, cc * 512:(cc + 1) * 512], ps[4 + (cc % 2)][:, :], AF.Identity,
                bias=bkup[:, h:h + 1], scale=constv[:, 5:6]),
                reads=[pb, buf("bkup_t"), buf("cvec")], writes=[buf(f"KT{par}")],
                tag=f"ke{h}_{cc}")
        for kq in range(8):
            pb = psB[4 + (kq % 2)]
            def mmv(e, kq=kq, par=par):
                r = None
                for j in range(4):
                    ks = kq * 4 + j
                    for c in range(4):
                        r = e.matmul(ps[4 + (kq % 2)][:, j * 128:(j + 1) * 128],
                                     kvlat[c][:, ks * 128:(ks + 1) * 128],
                                     kvwb[par][:, 512 + c * 128:512 + (c + 1) * 128],
                                     start=(c == 0), stop=(c == 3))
                return r
            g.op("T", mmv, reads=[buf(f"kvwb{par}"), buf("kvlat")], writes=[pb],
                 tag=f"vg{h}_{kq}")
            def evv(e, kq=kq, par=par):
                r = None
                for j in range(4):
                    ks = kq * 4 + j
                    r = e.tensor_copy(Vt[par][:, ks * VSTRIDE:ks * VSTRIDE + 128],
                                      ps[4 + (kq % 2)][:, j * 128:(j + 1) * 128])
                return r
            g.op("V", evv, reads=[pb], writes=[buf(f"Vt{par}")], tag=f"ve{h}_{kq}")

    def attend(h):
        par = h % 2
        hp = h % 2
        qpar = (h // 2) % 2
        nope = qt[qpar][hp]
        ropeq = qt[qpar][2]
        for qc in range(4):
            sc_ops = []
            for kt in range(32):
                sbB = psB[kt % 2]
                def mms(e, kt=kt, qc=qc):
                    e.matmul(ps[kt % 2][:, :], KT[par][:, kt * 128:(kt + 1) * 128],
                             nope[:, qc * 512:(qc + 1) * 512], start=True, stop=False)
                    return e.matmul(ps[kt % 2][:, :],
                                    kvrope[hp * 64:hp * 64 + 64,
                                           kt * 128:(kt + 1) * 128],
                                    ropeq[hp * 64:hp * 64 + 64,
                                          qc * 512:(qc + 1) * 512],
                                    start=False, stop=True)
                g.op("T", mms, reads=[buf(f"KT{par}"), buf("kvlat"),
                                      buf(f"qt{qpar}")], writes=[sbB],
                     tag=f"s{h}_{qc}_{kt}")
                g.op("A", lambda e, kt=kt: e.activation(pt[kt % 2][:, :],
                                                        ps[kt % 2][:, :],
                                                        AF.Exp,
                                                        bias=constv[:, 4:5],
                                                        scale=constv[:, 2:3]),
                     reads=[sbB, buf("cvec")], writes=[buf(f"pt{kt % 2}")],
                     tag=f"x{h}_{qc}_{kt}")
                if kt > 0:
                    emit_pv(h, qc, kt - 1, par)
            emit_pv(h, qc, 31, par)
            for qs in range(4):
                ob = psB[2 + qs]
                g.op("V", lambda e, qs=qs: e.reciprocal(rcp[:, qs:qs + 1],
                                                        ps[2 + qs][:, 128:129]),
                     reads=[ob], writes=[buf("rcp")], tag=f"r{h}_{qc}_{qs}")
                col = (qc * 4 + qs) * 128
                g.op("V", lambda e, qs=qs, col=col: e.scalar_tensor_tensor(
                    osum[:, col:col + 128], ps[2 + qs][:, 0:128], rcp[:, qs:qs + 1],
                    osum[:, col:col + 128], ALU.mult, ALU.add),
                    reads=[ob, buf("rcp"), buf("osum")], writes=[buf("osum")],
                    tag=f"o{h}_{qc}_{qs}")

    def emit_pv(h, qc, kt, par):
        for qs in range(4):
            ob = psB[2 + qs]
            g.op("T", lambda e, kt=kt, qs=qs: e.matmul(
                ps[2 + qs][:, 0:129], pt[kt % 2][:, qs * 128:(qs + 1) * 128],
                Vt[par][:, kt * VSTRIDE:kt * VSTRIDE + 129],
                start=(kt == 0), stop=(kt == 31)),
                reads=[buf(f"pt{kt % 2}"), buf(f"Vt{par}")], writes=[ob],
                tag=f"p{h}_{qc}_{kt}_{qs}")

    gen_pair(0)
    gen_head(0)
    for h in range(NH):
        if h % 2 == 1 and h + 1 < NH:
            gen_pair((h + 1) // 2)
        if h + 1 < NH:
            gen_head(h + 1)
        attend(h)

    if tap == "osum":
        f = dma("P", t_os, osum[:, :], reads=[buf("osum")], tag="tos")
        g.op("P", lambda e: e.nop(), deps=[f], tag="fin")
        return nc, g

    g.barrier()

    g.op("V", lambda e: e.tensor_copy(osT[:, :], osum[:, :]),
         reads=[buf("osum")], writes=[buf("osT")], tag="osc")
    for s4 in range(2):
        pb = psbB[s4]
        def trs(e, s4=s4):
            r = None
            for k in range(8):
                sidx = s4 * 8 + k
                r = e.transpose(psb[s4][:, k * 128:(k + 1) * 128],
                                osT[:, sidx * 128:(sidx + 1) * 128], ident[:, :])
            return r
        g.op("T", trs, reads=[buf("osT"), buf("ident")], writes=[pb], tag=f"ot{s4}")
    for s4 in range(2):
        pb = psbB[s4]
        def eve(e, s4=s4):
            r = None
            for k in range(8):
                sidx = s4 * 8 + k
                r = e.tensor_copy(osT[:, sidx * 128:(sidx + 1) * 128],
                                  psb[s4][:, k * 128:(k + 1) * 128])
            return r
        g.op("V", eve, reads=[pb], writes=[buf("osT")], tag=f"oe{s4}")

    dwo = dma("S", wostg[:, :], dr["WoT"],
              writes=[buf("xstrip"), buf("ostrip"), buf("osum")], tag="wol")
    g.op("V", lambda e: e.tensor_copy(WoTb[:, :], wostg[:, :]),
         reads=[buf("xstrip"), buf("ostrip")], writes=[buf("WoTb")], deps=[dwo],
         tag="woc")
    g.op("V", lambda e: e.tensor_reduce(Bv[:, :], bvupT[:, :], AX.X, ALU.add),
         reads=[buf("bvupT")], writes=[buf("Bv")], tag="bv")
    g.op("V", lambda e: e.tensor_scalar(osT[:, :], osT[:, :], Bv[:, 0:1], None,
                                        ALU.add),
         reads=[buf("Bv"), buf("osT")], writes=[buf("osT")], tag="bvadd")
    def bcast_tile(dst, dstb, src_dr, tag):
        for hh in range(2):
            dl = dma("S", xstrip[:, :], src_dr[:, hh * 2048:(hh + 1) * 2048],
                     writes=[buf("xstrip")], tag=f"{tag}l{hh}")
            g.op("V", lambda e, hh=hh: e.tensor_copy(
                dst[:, hh * 2048:(hh + 1) * 2048], xstrip[:, :]),
                reads=[buf("xstrip")], writes=[dstb], deps=[dl], tag=f"{tag}c{hh}")

    bcast_tile(bobc, buf("bobc"), dr["bo_tile"], "bo")
    bcast_tile(fbbc, buf("fbbc"), dr["fb_tile"], "fb")

    def p3_half(half):
        for s in range(8):
            sidx = half * 8 + s
            tok0 = sidx * 128
            xb = buf("xt3")
            dx = dma("P", xt3[:, :], xkv[tok0:tok0 + 128, :], writes=[xb],
                     tag=f"Cx{sidx}")
            g.op("V", lambda e: e.tensor_tensor(xt3[:, :], xt3[:, :], bobc[:, :],
                                                ALU.add),
                 reads=[xb, buf("bobc")], writes=[xb], deps=[dx], tag=f"Cb{sidx}")
            for occ in range(8):
                pb = psB[occ % 4]
                def mmo(e, occ=occ, sidx=sidx):
                    return e.matmul(ps[occ % 4][:, :],
                                    osT[:, sidx * 128:(sidx + 1) * 128],
                                    WoTb[:, occ * 512:(occ + 1) * 512],
                                    start=True, stop=True)
                g.op("T", mmo, reads=[buf("osT"), buf("WoTb")], writes=[pb],
                     tag=f"w{sidx}_{occ}")
                g.op("A", lambda e, occ=occ: e.activation(
                    xn[:, occ * 512:(occ + 1) * 512], ps[occ % 4][:, :], AF.Identity,
                    bias=constv[:, 4:5], scale=constv[:, 5:6]),
                    reads=[pb, buf("cvec")], writes=[buf("xn")], tag=f"wa{sidx}_{occ}")
                g.op("V", lambda e, occ=occ: e.tensor_tensor(
                    xn[:, occ * 512:(occ + 1) * 512], xn[:, occ * 512:(occ + 1) * 512],
                    xt3[:, occ * 512:(occ + 1) * 512], ALU.add),
                    reads=[xb, buf("xn")], writes=[buf("xn")], deps=[dx],
                    tag=f"we{sidx}_{occ}")
            xnb = buf("xn")
            dxn = dma("P", xnew_hbm[tok0:tok0 + 128, :], xn[:, :], reads=[xnb],
                      tag=f"xw{sidx}")
            if tap == "xnew":
                dma("P", t_xn[tok0:tok0 + 128, :], xn[:, :], reads=[xnb],
                    tag=f"txn{sidx}")
            ln_tile(xn[:, :], xnb, lntm2, buf("lntm2"), f"C{sidx}")
            transposes(lntm2, buf("lntm2"), s, lnfm2, buf("lnfm2"), 1024,
                       f"Ct{sidx}")
        return

    NPAN = 16
    out_dmas = []

    def p4_half(half):
        for piece in range(4):
            dwa = dma("S", xstrip[:, :].rearrange("p (a w) -> p a w", a=8),
                      dr["A_t"][piece * 8:(piece + 1) * 8].rearrange(
                          "a p m w -> p a (m w)"),
                      writes=[buf("xstrip")], tag=f"al{half}_{piece}")
            g.op("V", lambda e, piece=piece: e.tensor_copy(
                ATb[:, piece * 2048:(piece + 1) * 2048], xstrip[:, :]),
                reads=[buf("xstrip")], writes=[ZW01], deps=[dwa],
                tag=f"ac{half}_{piece}")
        for m in range(2):
            for qc2 in range(2):
                pb = psB[m * 2 + qc2]
                def mma(e, m=m, qc2=qc2):
                    r = None
                    for c in range(32):
                        r = e.matmul(
                            ps[m * 2 + qc2][:, :],
                            ATb[:, c * 256 + m * 128:c * 256 + (m + 1) * 128],
                            lnfm2[:, c * 1024 + qc2 * 512:c * 1024 + (qc2 + 1) * 512],
                            start=(c == 0), stop=(c == 31))
                    return r
                g.op("T", mma, reads=[ZW01, buf("lnfm2")], writes=[pb],
                     tag=f"xa{half}_{m}_{qc2}")
                g.op("V", lambda e, m=m, qc2=qc2: e.tensor_scalar(
                    xa[:, m * 1024 + qc2 * 512:m * 1024 + (qc2 + 1) * 512],
                    ps[m * 2 + qc2][:, :], wvec[:, m:m + 1], None, ALU.mult),
                    reads=[pb, buf("wvec")], writes=[buf("xa")],
                    tag=f"xe{half}_{m}_{qc2}")
        if half == 0:
            for c2 in range(2):
                for hh in range(2):
                    dwb = dma("S", xstrip[:, :],
                              dr["Bf_t"][c2][:, hh * 2048:(hh + 1) * 2048],
                              writes=[buf("xstrip")], tag=f"bl{c2}_{hh}")
                    g.op("V", lambda e, c2=c2, hh=hh: e.tensor_copy(
                        BfTb[:, c2 * 4096 + hh * 2048:c2 * 4096 + (hh + 1) * 2048],
                        xstrip[:, :]),
                        reads=[buf("xstrip")], writes=[buf("BfTb")], deps=[dwb],
                        tag=f"bc{c2}_{hh}")
        for p in range(NPAN):
            xsb = buf("xstrip")
            for qd in range(4):
                dwc = dma("S", xstrip[:, :], dr["WffnT_t"][p][qd],
                          writes=[xsb], tag=f"fl{half}_{p}_{qd}")
                g.op("V", lambda e, qd=qd, p=p: e.tensor_copy(
                    wpan[p % 2][:, qd * 2048:(qd + 1) * 2048], xstrip[:, :]),
                    reads=[xsb], writes=[ZW01], deps=[dwc],
                    tag=f"fc{half}_{p}_{qd}")
            dxs = dma("P", xstrip[:, :].rearrange("q (s w) -> q s w", s=8),
                      xnew_hbm[half * 1024:(half + 1) * 1024,
                               p * 256:(p + 1) * 256].rearrange(
                          "(s q) w -> q s w", q=128),
                      writes=[xsb], tag=f"xs{half}_{p}")
            g.op("V", lambda e, p=p: e.tensor_tensor(
                xstrip[:, :].rearrange("q (s w) -> q s w", s=8),
                xstrip[:, :].rearrange("q (s w) -> q s w", s=8),
                bass.AP(fbbc, p * 256, [[4096, 128], [0, 8], [1, 256]]),
                ALU.add),
                reads=[xsb, buf("fbbc")], writes=[xsb], deps=[dxs],
                tag=f"xb{half}_{p}")
            for s in range(8):
                pb = psB[4 + (s % 2)]
                def mmw(e, p=p, s=s):
                    for c in range(32):
                        e.matmul(ps[4 + (s % 2)][:, 0:256],
                                 lnfm2[:, c * 1024 + s * 128:c * 1024 + (s + 1) * 128],
                                 wpan[p % 2][:, c * 256:(c + 1) * 256],
                                 start=(c == 0), stop=False)
                    r = None
                    for c2 in range(2):
                        r = e.matmul(ps[4 + (s % 2)][:, 0:256],
                                     xa[:, c2 * 1024 + s * 128:c2 * 1024 + (s + 1) * 128],
                                     BfTb[:, c2 * 4096 + p * 256:c2 * 4096 + (p + 1) * 256],
                                     start=False, stop=(c2 == 1))
                    return r
                g.op("T", mmw, reads=[buf("lnfm2"), ZW01, buf("xa"),
                                      buf("BfTb")],
                     writes=[pb], tag=f"f{half}_{p}_{s}")
                g.op("A", lambda e, p=p, s=s: e.activation(
                    ostrip[:, s * 256:(s + 1) * 256], ps[4 + (s % 2)][:, 0:256],
                    AF.Identity, bias=constv[:, 4:5], scale=constv[:, 5:6]),
                    reads=[pb, buf("cvec")], writes=[buf("ostrip")],
                    tag=f"fa{half}_{p}_{s}")
                g.op("V", lambda e, p=p, s=s: e.tensor_tensor(
                    ostrip[:, s * 256:(s + 1) * 256], ostrip[:, s * 256:(s + 1) * 256],
                    xstrip[:, s * 256:(s + 1) * 256], ALU.add),
                    reads=[xsb, buf("ostrip")], writes=[buf("ostrip")], deps=[dxs],
                    tag=f"fe{half}_{p}_{s}")
            do = dma("P", out[half * 1024:(half + 1) * 1024,
                              p * 256:(p + 1) * 256].rearrange(
                "(s q) w -> q s w", q=128),
                ostrip[:, :].rearrange("q (s w) -> q s w", s=8),
                reads=[buf("ostrip")], tag=f"ow{half}_{p}")
            out_dmas.append(do)

    for half in range(2):
        p3_half(half)
        if tap == "xnew":
            continue
        p4_half(half)

    if tap == "xnew":
        g.op("P", lambda e: e.nop(), deps=[g.last["dP"]], tag="fin")
        return nc, g

    g.op("P", lambda e: e.nop(), deps=out_dmas, tag="fin")
    return nc, g


def finish(nc, g):
    with (
        nc.semaphore("semT") as sT, nc.semaphore("semV") as sV,
        nc.semaphore("semA") as sA, nc.semaphore("semP") as sP,
        nc.semaphore("semS") as sS, nc.semaphore("semdS") as sdS,
        nc.semaphore("semdP") as sdP,
        nc.Block() as block,
    ):
        sems = {"T": sT, "V": sV, "A": sA, "P": sP, "S": sS, "dS": sdS, "dP": sdP}
        run = g.emit_streams(sems)

        @block.tensor
        def _(e):
            run("T", e)

        @block.vector
        def _(e):
            run("V", e)

        @block.scalar
        def _(e):
            run("A", e)

        @block.gpsimd
        def _(e):
            run("P", e)

        @block.sync
        def _(e):
            run("S", e)
    return nc


_PROG = {}


def get_program(tap=None):
    if tap not in _PROG:
        nc, g = build_program(tap)
        _PROG[tap] = finish(nc, g)
    return _PROG[tap]


def make_in_maps(inputs):
    x = np.asarray(inputs["L0_attn_hidden_size"])
    w = host_layout(inputs)
    in_maps = []
    for c in range(8):
        b, half = c // 2, c % 2
        xs = np.ascontiguousarray(np.roll(x[b], -half * OWN, axis=0))
        m = {"xkv": xs}
        m.update(w)
        in_maps.append(m)
    return in_maps


def kernel(**inputs):
    nc = get_program(None)
    in_maps = make_in_maps(inputs)
    res = run_bass_kernel_spmd(nc, in_maps, core_ids=list(range(8)))
    outv = np.empty((4, SEQ, DM), dtype=np.float32)
    for c in range(8):
        b, half = c // 2, c % 2
        outv[b, half * OWN:(half + 1) * OWN] = res.results[c]["out"]
    return outv


# version 04
# speedup vs baseline: 1.1637x; 1.1637x over previous
import math
import numpy as np
import ml_dtypes
import concourse.bass as bass
import concourse.mybir as mybir
from concourse.bass_utils import run_bass_kernel_spmd

F32 = mybir.dt.float32
BF16 = mybir.dt.bfloat16
AF = mybir.ActivationFunctionType
ALU = mybir.AluOpType
AX = mybir.AxisListType
BF = ml_dtypes.bfloat16

NH = 32
DM = 4096
SEQ = 4096
OWN = 2048
NEPS = 1e-5
SCALE = 1.0 / math.sqrt(192.0)
VSTRIDE = 132
KB = 1024



class Buf:
    __slots__ = ("name", "writers", "readers")

    def __init__(self, name):
        self.name = name
        self.writers = {}
        self.readers = []


class Op:
    __slots__ = ("eng", "fn", "deps", "dma", "used", "val", "tag")

    def __init__(self, eng, fn, dma, tag):
        self.eng, self.fn, self.dma, self.tag = eng, fn, dma, tag
        self.deps = []
        self.used = False
        self.val = None


def sem_of(o):
    return ("d" + o.eng) if o.dma else o.eng


class G:
    def __init__(self):
        self.ops = []
        self.bufs = []
        self.last = {}

    def newbuf(self, name):
        b = Buf(name)
        self.bufs.append(b)
        return b

    def op(self, eng, fn, reads=(), writes=(), deps=(), dma=False, tag=""):
        o = Op(eng, fn, dma, tag)
        d = []
        for b in reads:
            d.extend(b.writers.values())
        for b in writes:
            d.extend(b.writers.values())
            d.extend(b.readers)
        for b in reads:
            b.readers.append(o)
        for b in writes:
            b.writers = dict(b.writers)
            b.writers[sem_of(o)] = o
            b.readers = []
        d.extend(x for x in deps if x is not None)
        seen = set()
        for x in d:
            if id(x) not in seen and x is not o:
                seen.add(id(x))
                o.deps.append(x)
        self.ops.append(o)
        self.last[sem_of(o)] = o
        return o

    def barrier(self):
        lasts = [self.last[k] for k in list(self.last)]
        bops = {}
        for eng in ("T", "V", "A", "P", "S"):
            bops[eng] = self.op(eng, lambda e: e.nop(), deps=lasts, tag=f"bar_{eng}")
        all_b = list(bops.values())
        for eng in ("T", "V", "A", "P", "S"):
            self.op(eng, lambda e: e.nop(), deps=all_b, tag=f"bar2_{eng}")
        for b in self.bufs:
            b.writers = {"T": self.last["T"]}
            b.readers = []

    @staticmethod
    def _skip(d, o):
        if d.eng == "T" and o.eng == "T" and not d.dma and not o.dma:
            return True
        if d.dma and o.dma and d.eng == o.eng:
            return True
        return False

    def emit_streams(self, sems):
        for o in self.ops:
            for d in o.deps:
                if not self._skip(d, o):
                    d.used = True
        counters = {}
        for o in self.ops:
            if o.used or o.dma:
                s = sem_of(o)
                counters[s] = counters.get(s, 0) + (16 if o.dma else 1)
                o.val = counters[s]
        waited = {}

        def run(eng_key, eng_obj):
            for o in self.ops:
                if o.eng != eng_key:
                    continue
                for d in o.deps:
                    if self._skip(d, o):
                        continue
                    s = sem_of(d)
                    v = d.val
                    assert v is not None, (d.tag, o.tag)
                    if waited.get((eng_key, s), 0) >= v:
                        continue
                    eng_obj.wait_ge(sems[s], v)
                    waited[(eng_key, s)] = v
                inst = o.fn(eng_obj)
                if o.used or o.dma:
                    assert inst is not None, o.tag
                    inst.then_inc(sems[sem_of(o)], 16 if o.dma else 1)

        return run




def host_layout(inputs):
    d = {}
    Wq_c = np.asarray(inputs["L0_attn_Wq_c"])
    Wq_up = np.asarray(inputs["L0_attn_Wq_up"])
    Wkv_c = np.asarray(inputs["L0_attn_Wkv_c"])
    Wkv_up = np.asarray(inputs["L0_attn_Wkv_up"])
    Wo = np.asarray(inputs["L0_attn_Wo"])
    W = np.asarray(inputs["L0_ffn_W"])
    A = np.asarray(inputs["L0_ffn_A"])
    Bf = np.asarray(inputs["L0_ffn_Bf"])

    d["Wqc_t"] = np.ascontiguousarray(
        Wq_c.reshape(12, 128, 32, 128).transpose(0, 3, 2, 1))
    d["Wkvc_t"] = np.ascontiguousarray(
        Wkv_c[:512].reshape(4, 128, 32, 128).transpose(0, 3, 2, 1))
    d["Wkvc_r"] = np.ascontiguousarray(
        Wkv_c[512:].reshape(64, 32, 128).transpose(2, 1, 0))

    Wqu = Wq_up.reshape(16, 2, 192, 1536)
    per_pair = np.concatenate(
        [Wqu[:, 0, :128], Wqu[:, 1, :128], Wqu[:, 0, 128:], Wqu[:, 1, 128:]], axis=1)
    d["Wqup_t"] = np.ascontiguousarray(
        per_pair.reshape(16, 384, 12, 128).transpose(0, 3, 2, 1))
    bqu = np.asarray(inputs["L0_attn_bq_up"]).reshape(16, 2, 192)
    bqu_r = np.concatenate(
        [bqu[:, 0, :128], bqu[:, 1, :128], bqu[:, 0, 128:], bqu[:, 1, 128:]], axis=1)
    d["bqup_t"] = np.ascontiguousarray(
        bqu_r.reshape(16, 3, 128).transpose(2, 0, 1).reshape(128, 48))

    Wku = Wkv_up.reshape(32, 2, 128, 512)
    d["Wkup_t"] = np.ascontiguousarray(
        Wku[:, 0].reshape(32, 128, 4, 128).transpose(0, 3, 2, 1))
    d["Wvup_t"] = np.ascontiguousarray(
        Wku[:, 1].reshape(32, 128, 4, 128).transpose(0, 3, 2, 1))
    bku = np.asarray(inputs["L0_attn_bkv_up"]).reshape(32, 2, 128)
    d["bkup_t"] = np.ascontiguousarray(bku[:, 0].T)
    d["bvupT"] = np.ascontiguousarray(bku[:, 1].T)

    d["WoT"] = np.ascontiguousarray(Wo.T)
    d["WffnT"] = np.ascontiguousarray(W.T)
    d["WffnT_t"] = np.ascontiguousarray(
        W.T.reshape(4, 8, 128, 16, 256).transpose(3, 0, 2, 1, 4))
    Afl = A.reshape(256, 4096)
    d["A_t"] = np.ascontiguousarray(
        Afl.reshape(2, 128, 32, 128).transpose(2, 3, 0, 1))
    d["Bf_t"] = np.ascontiguousarray(
        Bf.transpose(0, 2, 1).reshape(256, 4096).reshape(2, 128, 4096))

    d["bqc_t"] = np.ascontiguousarray(
        np.asarray(inputs["L0_attn_bq_c"]).reshape(12, 128).T)
    d["bkvc_t"] = np.ascontiguousarray(
        np.pad(np.asarray(inputs["L0_attn_bkv_c"]), (0, 64)).reshape(5, 128).T)
    d["bo_row"] = np.ascontiguousarray(
        np.asarray(inputs["L0_attn_bo"]).reshape(1, 4096))
    d["fb_row"] = np.ascontiguousarray(
        np.asarray(inputs["L0_ffn_bias"]).reshape(1, 4096))
    d["bo_tile"] = np.ascontiguousarray(
        np.tile(np.asarray(inputs["L0_attn_bo"]).reshape(1, 4096), (128, 1)))
    d["fb_tile"] = np.ascontiguousarray(
        np.tile(np.asarray(inputs["L0_ffn_bias"]).reshape(1, 4096), (128, 1)))
    d["coeffs"] = np.ascontiguousarray(np.asarray(inputs["L0_ffn_coeffs"]).reshape(16, 1))
    d["scales"] = np.ascontiguousarray(np.asarray(inputs["L0_ffn_scales"]).reshape(16, 1))

    d["ident"] = np.eye(128, dtype=BF)
    d["ones1"] = np.ones((1, 128), dtype=BF)
    d["cvec"] = np.tile(np.array([[1e-5, -0.5, 1.0 / math.sqrt(192.0),
                                   1.0 / 4096.0, 0.0, 1.0]],
                                 dtype=np.float32), (128, 1))
    sel = np.zeros((16, 256), dtype=np.float32)
    for col in range(256):
        sel[col // 16, col] = 1.0
    d["sel"] = sel
    return d




DBG_SKIP = set()


def build_program(tap=None):
    nc = bass.Bass("TRN2", target_bir_lowering=False)
    g = G()

    xkv = nc.dram_tensor("xkv", [SEQ, DM], F32, kind="ExternalInput").ap()
    dr = {}
    for name, shape, dt in [
        ("Wqc_t", [12, 128, 32, 128], F32), ("Wkvc_t", [4, 128, 32, 128], F32),
        ("Wkvc_r", [128, 32, 64], F32), ("Wqup_t", [16, 128, 12, 384], F32),
        ("Wkup_t", [32, 128, 4, 128], F32), ("Wvup_t", [32, 128, 4, 128], F32),
        ("bkup_t", [128, 32], F32), ("bvupT", [128, 32], F32),
        ("bqup_t", [128, 48], F32),
        ("WoT", [128, 4096], F32), ("WffnT", [4096, 4096], F32),
        ("WffnT_t", [16, 4, 128, 8 * 256], F32),
        ("A_t", [32, 128, 2, 128], F32), ("Bf_t", [2, 128, 4096], F32),
        ("bqc_t", [128, 12], F32), ("bkvc_t", [128, 5], F32),
        ("bo_row", [1, 4096], F32), ("fb_row", [1, 4096], F32),
        ("bo_tile", [128, 4096], F32), ("fb_tile", [128, 4096], F32),
        ("coeffs", [16, 1], F32), ("scales", [16, 1], F32),
        ("ident", [128, 128], BF16), ("ones1", [1, 128], BF16),
        ("cvec", [128, 6], F32),
        ("sel", [16, 256], F32),
    ]:
        dr[name] = nc.dram_tensor(name, shape, dt, kind="ExternalInput").ap()

    out = nc.dram_tensor("out", [OWN, DM], F32, kind="ExternalOutput").ap()
    xnew_hbm = nc.dram_tensor("xnew_scr", [OWN, DM], F32, kind="Internal").ap()
    t_kv = t_q = t_os = t_xn = None
    if tap == "lat":
        t_kv = nc.dram_tensor("t_kv", [5 * 128, SEQ], F32, kind="ExternalOutput").ap()
        t_q = nc.dram_tensor("t_q", [12 * 128, OWN], F32, kind="ExternalOutput").ap()
    if tap == "osum":
        t_os = nc.dram_tensor("t_os", [128, 16 * 128], F32, kind="ExternalOutput").ap()
    if tap == "xnew":
        t_xn = nc.dram_tensor("t_xn", [OWN, DM], F32, kind="ExternalOutput").ap()

    SBUF_BASE = 16384
    SBUF_END = 229247
    cur = [SBUF_BASE + 0]

    def zalloc(name, shape, dt, at=None):
        nbytes = int(np.prod(shape[1:])) * (4 if dt == F32 else 2) if len(shape) > 1 \
            else 4
        if len(shape) == 2:
            nbytes = shape[1] * (4 if dt == F32 else 2)
        if at is None:
            at = (cur[0] + 31) & ~31
            cur[0] = at + nbytes
        t = nc.alloc_sbuf_tensor_at(name, shape, dt, offset=at)
        return t, at, at + nbytes

    ident, _, _ = zalloc("ident", [128, 128], BF16)
    ones1, _, _ = zalloc("ones1", [1, 128], BF16)
    bqc, _, _ = zalloc("bqc", [128, 12], F32)
    bkvc, _, _ = zalloc("bkvc", [128, 5], F32)
    bqup, _, _ = zalloc("bqup", [128, 48], F32)
    bkup, _, _ = zalloc("bkup", [128, 32], F32)
    bvupT, _, _ = zalloc("bvupT", [128, 32], F32)
    selw, _, _ = zalloc("selw", [16, 256], F32)
    cfs, _, _ = zalloc("cfs", [16, 2], F32)
    w16, _, _ = zalloc("w16", [16, 1], F32)
    wvec, _, _ = zalloc("wvec", [128, 2], F32)
    cur[0] = (cur[0] + 255) & ~255
    red, _, _ = zalloc("red", [128, 12], F32)
    cur[0] = (cur[0] + 255) & ~255
    ssq, _, _ = zalloc("ssq", [128, 8], F32)
    rcp, _, _ = zalloc("rcp", [128, 4], F32)
    Bv, _, _ = zalloc("Bv", [128, 1], F32)
    Bvb, _, _ = zalloc("Bvb", [128, 1], BF16)
    constv, _, _ = zalloc("constv", [128, 6], F32)
    Z0_END = (cur[0] + 31) & ~31
    SB_CAP = nc.sbuf_bytes_remaining + 0
    BASE = Z0_END

    def at(off, name, shape, dt, end_limit=SBUF_END):
        t, a, e = zalloc(name, shape, dt, at=off)
        assert e <= end_limit, (name, e)
        return t

    kvlat = [at(BASE + i * 8 * KB, f"kvlat{i}", [128, SEQ], BF16) for i in range(4)]
    kvrope = at(BASE + 32 * KB, "kvrope", [128, SEQ], BF16)
    QBASE = BASE + 40 * KB
    qlat = [at(QBASE + m * 4 * KB, f"qlat{m}", [128, OWN], BF16) for m in range(12)]
    R3 = QBASE + 48 * KB

    lnfm = at(R3, "lnfm", [128, 32 * 512], BF16)
    xt = [at(R3 + 32 * KB, "xt0", [128, DM], F32),
          at(R3 + 48 * KB, "xt1", [128, DM], F32)]
    wbf = [at(R3 + 48 * KB, "wbf0", [128, 4096], BF16),
           at(R3 + 56 * KB, "wbf1", [128, 4096], BF16)]
    lntm = at(R3 + 64 * KB, "lntm", [128, DM], BF16)
    wstg32 = [at(R3 + 72 * KB, "wstg32_0", [128, 4096], F32),
              at(R3 + 88 * KB, "wstg32_1", [128, 4096], F32)]
    kvcb = at(QBASE, "kvcb", [128, 32 * 576], BF16)
    CW = 576

    KT = [at(R3, "KT0", [128, SEQ], BF16), at(R3 + 8 * KB, "KT1", [128, SEQ], BF16)]
    Vt = [at(R3 + 16 * KB, "Vt0", [128, 32 * VSTRIDE], BF16),
          at(R3 + 25 * KB, "Vt1", [128, 32 * VSTRIDE], BF16)]
    qt = [[at(R3 + 34 * KB + (i * 3 + m) * 4 * KB, f"qt{i}_{m}", [128, OWN], BF16)
           for m in range(3)] for i in range(2)]
    pt = [at(R3 + 58 * KB, "pt0", [128, 512], BF16),
          at(R3 + 59 * KB, "pt1", [128, 512], BF16)]
    kvw32 = [at(R3 + 60 * KB, "kvw32_0", [128, 1024], F32),
             at(R3 + 64 * KB, "kvw32_1", [128, 1024], F32)]
    kvwb = [at(R3 + 68 * KB, "kvwb0", [128, 1024], BF16),
            at(R3 + 70 * KB, "kvwb1", [128, 1024], BF16)]
    quw32 = at(R3 + 72 * KB, "quw32", [128, 12 * 384], F32)
    quwb = at(R3 + 90 * KB, "quwb", [128, 12 * 384], BF16)
    osum = at(R3 + 99 * KB, "osum", [128, 16 * 128], F32)
    kvropeB = at(R3 + 107 * KB, "kvropeB", [128, SEQ], BF16)

    lnfm2 = at(BASE, "lnfm2", [128, 32 * 1024], BF16)
    P3B = BASE + 64 * KB
    xn = at(P3B, "xn", [128, DM], F32)
    xt3 = at(P3B + 16 * KB, "xt3", [128, DM], F32)
    lntm2 = at(P3B + 32 * KB, "lntm2", [128, DM], BF16)
    osT = at(P3B + 40 * KB, "osT", [128, 16 * 128], BF16)
    WoTb = at(P3B + 44 * KB, "WoTb", [128, 4096], BF16)
    bobc = at(P3B + 52 * KB, "bobc", [128, 4096], BF16)
    fbbc = at(P3B + 60 * KB, "fbbc", [128, 4096], BF16)
    xa = at(P3B + 68 * KB, "xa", [128, 2 * 1024], BF16)
    BfTb = at(P3B + 72 * KB, "BfTb", [128, 2 * 4096], BF16)
    wpan = [at(P3B + 88 * KB, "wpan0", [128, 32 * 256], BF16),
            at(P3B + 104 * KB, "wpan1", [128, 32 * 256], BF16)]
    ATb = at(P3B + 88 * KB, "ATb", [128, 32 * 256], BF16)
    borow = at(P3B + 88 * KB, "borow", [1, 4096], F32)
    brow2 = at(P3B + 104 * KB, "brow2", [1, 4096], F32)
    xstrip = at(P3B + 122 * KB, "xstrip", [128, 8 * 256], F32)
    ostrip = at(P3B + 130 * KB, "ostrip", [128, 8 * 256], F32)
    wostg = at(P3B + 122 * KB, "wostg", [128, 4096], F32)

    ps = [nc.alloc_psum_tensor(f"ps{i}", [128, 512], F32) for i in range(6)]
    psb = [nc.alloc_psum_tensor(f"psb{i}", [128, 1024], BF16) for i in range(2)]

    bufs = {}

    def buf(name):
        if name not in bufs:
            bufs[name] = g.newbuf(name)
        return bufs[name]

    psB = [buf(f"ps{i}") for i in range(6)]
    psbB = [buf(f"psb{i}") for i in range(2)]
    ZW01 = buf("ZW01")

    def dma(engq, dst_ap, src_ap, reads=(), writes=(), deps=(), tag=""):
        return g.op(engq, lambda e, d=dst_ap, s=src_ap: e.dma_start(d, s),
                    reads=reads, writes=writes, deps=deps, dma=True, tag=tag)

    _cl = [("ident", ident), ("ones1", ones1), ("bqc_t", bqc),
           ("bkvc_t", bkvc), ("bqup_t", bqup), ("bkup_t", bkup),
           ("bvupT", bvupT), ("sel", selw), ("cvec", constv)]
    if "consts" in DBG_SKIP:
        _cl = [("ident", ident), ("cvec", constv)]
    for name, dst in _cl:
        dma("S", dst[:, :] if name != "ones1" else dst[0:1, :], dr[name],
            writes=[buf(name)], tag=f"ld_{name}")
    dma("S", cfs[:, 0:1], dr["coeffs"], writes=[buf("cfs")], tag="ld_cf")
    dma("S", cfs[:, 1:2], dr["scales"], writes=[buf("cfs")], tag="ld_sc")
    for wi in range(4):
        g.op("A", lambda e, wi=wi: e.activation(red[:, 9:10], constv[:, 0:1],
                                                [AF.Identity, AF.Exp, AF.Ln,
                                                 AF.Identity][wi],
                                                bias=constv[:, 4:5],
                                                scale=constv[:, 5:6]),
             reads=[buf("cvec")], writes=[buf("warm")], tag=f"warm{wi}")
    g.op("V", lambda e: e.tensor_tensor(w16[:, :], cfs[:, 0:1], cfs[:, 1:2], ALU.mult),
         reads=[buf("cfs")], writes=[buf("w16")], tag="w16")
    if "wvec" not in DBG_SKIP:
        for m in range(2):
            g.op("T", lambda e, m=m: e.matmul(ps[5][:, m:m + 1],
                                              selw[:, m * 128:(m + 1) * 128],
                                              w16[:, :], start=True, stop=True),
                 reads=[buf("sel"), buf("w16")], writes=[psB[5]], tag=f"wvm{m}")
        sp = g.op("T", lambda e: e.matmul(ps[4][:, 0:1], selw[:, 0:128],
                                          w16[:, :], start=True, stop=True),
                  reads=[buf("sel"), buf("w16")], writes=[psB[4]], tag="wvsp")
        for m in range(2):
            g.op("A", lambda e, m=m: e.activation(wvec[:, m:m + 1], ps[5][:, m:m + 1],
                                                  AF.Identity, bias=constv[:, 4:5],
                                                  scale=constv[:, 5:6]),
                 reads=[psB[5], buf("cvec")], writes=[buf("wvec")], deps=[sp],
                 tag=f"wve{m}")

    def ln_tile(x_ap, xb, rstd_dst, dstb, tag):
        g.op("V", lambda e: e.tensor_reduce(red[:, 0:1], x_ap, AX.X, ALU.add,
                                            negate=True),
             reads=[xb], writes=[buf("red")], tag=f"{tag}rs")
        g.op("V", lambda e: e.scalar_tensor_tensor(rstd_dst[:, :], x_ap, 1.0, x_ap,
                                                   ALU.mult, ALU.mult,
                                                   accum_out=ssq[:, 0:1]),
             reads=[xb], writes=[buf("red"), dstb], tag=f"{tag}sq")
        g.op("V", lambda e: e.tensor_scalar(red[:, 2:3], red[:, 0:1], 1.0 / DM, None,
                                            ALU.mult),
             reads=[buf("red")], writes=[buf("red")], tag=f"{tag}nm")
        g.op("V", lambda e: e.tensor_tensor(red[:, 3:4], red[:, 2:3], red[:, 2:3],
                                            ALU.mult),
             reads=[buf("red")], writes=[buf("red")], tag=f"{tag}m2")
        g.op("V", lambda e: e.tensor_scalar(red[:, 4:5], red[:, 3:4], -1.0, NEPS,
                                            ALU.mult, ALU.add),
             reads=[buf("red")], writes=[buf("red")], tag=f"{tag}m2e")
        g.op("V", lambda e: e.scalar_tensor_tensor(red[:, 5:6], ssq[:, 0:1], 1.0 / DM,
                                                   red[:, 4:5], ALU.mult, ALU.add),
             reads=[buf("red")], writes=[buf("red")], tag=f"{tag}v")
        g.op("A", lambda e: e.activation(red[:, 6:7], red[:, 5:6], AF.Ln,
                                         bias=constv[:, 4:5], scale=constv[:, 5:6]),
             reads=[buf("red"), buf("cvec")], writes=[buf("red")], tag=f"{tag}l")
        g.op("A", lambda e: e.activation(red[:, 7:8], red[:, 6:7], AF.Exp,
                                         bias=constv[:, 4:5], scale=constv[:, 1:2]),
             reads=[buf("red"), buf("cvec")], writes=[buf("red")], tag=f"{tag}r")
        g.op("V", lambda e: e.tensor_scalar(rstd_dst[:, :], x_ap, red[:, 2:3],
                                            red[:, 7:8], ALU.add, ALU.mult),
             reads=[xb, buf("red")], writes=[dstb], tag=f"{tag}n")

    def transposes(src, srcb, tt, dstfm, dst_buf, wtok, tag):
        for cg in range(4):
            pb = psbB[cg % 2]
            def tr(e, cg=cg):
                r = None
                for k in range(8):
                    c = cg * 8 + k
                    r = e.transpose(psb[cg % 2][:, k * 128:(k + 1) * 128],
                                    src[:, c * 128:(c + 1) * 128], ident[:, :])
                return r
            g.op("T", tr, reads=[srcb, buf("ident")], writes=[pb],
                 tag=f"{tag}t{cg}")
            def ev(e, cg=cg):
                r = None
                for k in range(8):
                    c = cg * 8 + k
                    r = e.tensor_copy(
                        dstfm[:, c * wtok + tt * 128:c * wtok + (tt + 1) * 128],
                        psb[cg % 2][:, k * 128:(k + 1) * 128])
                return r
            g.op("V", ev, reads=[pb], writes=[dst_buf], tag=f"{tag}e{cg}")

    g.op("P", lambda e: e.memset(kvrope[64:128, :], 0.0), writes=[buf("kvlat")],
         tag="rz1")
    g.op("P", lambda e: e.memset(kvropeB[0:64, :], 0.0), writes=[buf("kvropeB")],
         tag="rz2")

    for m in range(4) if "kvc" not in DBG_SKIP else []:
        dw = dma("S", wstg32[m % 2][:, :], dr["Wkvc_t"][m],
                 writes=[buf(f"wstg32_{m % 2}")], tag=f"kvcl{m}")
        g.op("V", lambda e, m=m: e.tensor_copy(
            kvcb[:, :].rearrange("p (c w) -> p c w", c=32)[:, :, m * 128:(m + 1) * 128],
            wstg32[m % 2][:, :].rearrange("p (c w) -> p c w", c=32)),
            reads=[buf(f"wstg32_{m % 2}")], writes=[buf("kvcb")], deps=[dw],
            tag=f"kvcc{m}")
    dw = dma("S", wstg32[0][:, 0:32 * 64], dr["Wkvc_r"], writes=[buf("wstg32_0")],
             tag="kvclr") if "kvc" not in DBG_SKIP else None
    ("kvc" in DBG_SKIP) or g.op("V", lambda e: e.tensor_copy(
        kvcb[:, :].rearrange("p (c w) -> p c w", c=32)[:, :, 512:576],
        wstg32[0][:, 0:32 * 64].rearrange("p (c w) -> p c w", c=32)),
        reads=[buf("wstg32_0")], writes=[buf("kvcb")], deps=[dw], tag="kvccr")

    for grp in range(8):
        for tt in range(4):
            tok0 = grp * 512 + tt * 128
            xb = buf(f"xt{tt % 2}")
            dx = dma("S", xt[tt % 2][:, :], xkv[tok0:tok0 + 128, :], writes=[xb],
                     tag=f"Ax{grp}_{tt}")
            ln_tile(xt[tt % 2][:, :], xb, lntm, buf("lntm"), f"A{grp}_{tt}")
            if "tr" not in DBG_SKIP:
                transposes(lntm, buf("lntm"), tt, lnfm, buf("lnfm"), 512,
                           f"At{grp}_{tt}")
            if tap == "ln" and grp == 0 and tt == 0:
                t_ln = nc.dram_tensor("t_ln", [128, DM], BF16,
                                      kind="ExternalOutput").ap()
                t_fm = nc.dram_tensor("t_fm", [128, 32 * 512], BF16,
                                      kind="ExternalOutput").ap()
                t_id = nc.dram_tensor("t_id", [128, 128], BF16,
                                      kind="ExternalOutput").ap()
                t_red = nc.dram_tensor("t_red", [128, 12], F32,
                                       kind="ExternalOutput").ap()
                t_ssq = nc.dram_tensor("t_ssq", [128, 8], F32,
                                       kind="ExternalOutput").ap()
                t_x = nc.dram_tensor("t_x", [128, DM], F32,
                                     kind="ExternalOutput").ap()
                fin = [dma("P", t_ln, lntm[:, :], reads=[buf("lntm")], tag="tln"),
                       dma("P", t_fm, lnfm[:, :], reads=[buf("lnfm")], tag="tfm"),
                       dma("P", t_id, ident[:, :], reads=[buf("ident")], tag="tid"),
                       dma("P", t_red, red[:, :], reads=[buf("red")], tag="tred"),
                       dma("P", t_ssq, ssq[:, :], reads=[buf("red")], tag="tssq"),
                       dma("P", t_x, xt[0][:, :], reads=[buf("xt0")], tag="tx")]
                g.op("P", lambda e: e.nop(), deps=fin, tag="fin")
                return nc, g
        for m in range(5):
            mw = 128 if m < 4 else 64
            pb = psB[m % 2]
            def mmk(e, m=m, mw=mw):
                r = None
                for c in range(32):
                    r = e.matmul(ps[m % 2][0:mw, :],
                                 kvcb[:, c * CW + m * 128:c * CW + m * 128 + mw],
                                 lnfm[:, c * 512:(c + 1) * 512],
                                 start=(c == 0), stop=(c == 31))
                return r
            g.op("T", mmk, reads=[buf("kvcb"), buf("lnfm")], writes=[pb],
                 tag=f"Am{grp}_{m}")
            dst = kvlat[m] if m < 4 else kvrope
            g.op("A", lambda e, m=m, mw=mw, grp=grp, dst=dst: e.activation(
                dst[0:mw, grp * 512:(grp + 1) * 512], ps[m % 2][0:mw, :], AF.Identity,
                bias=bkvc[0:mw, m:m + 1], scale=constv[0:mw, 5:6]),
                reads=[pb, buf("bkvc_t"), buf("cvec")], writes=[buf("kvlat")],
                tag=f"Ae{grp}_{m}")

    dma("S", kvropeB[64:128, :], kvrope[0:64, :], reads=[buf("kvlat")],
        writes=[buf("kvropeB")], tag="ropedup")

    g.barrier()

    for qr in range(4):
        for tt in range(4):
            tok0 = qr * 512 + tt * 128
            xb = buf("xt0")
            dx = dma("P", xt[0][:, :], xkv[tok0:tok0 + 128, :], writes=[xb],
                     tag=f"Bx{qr}_{tt}")
            ln_tile(xt[0][:, :], xb, lntm, buf("lntm"), f"B{qr}_{tt}")
            transposes(lntm, buf("lntm"), tt, lnfm, buf("lnfm"), 512, f"Bt{qr}_{tt}")
        for m in range(12):
            wsb = buf(f"wstg32_{m % 2}")
            dw = dma("S", wstg32[m % 2][:, :], dr["Wqc_t"][m], writes=[wsb],
                     tag=f"Bw{qr}_{m}")
            cb = buf(f"wbf{m % 2}")
            g.op("V", lambda e, m=m: e.tensor_copy(wbf[m % 2][:, :],
                                                   wstg32[m % 2][:, :]),
                 reads=[wsb], writes=[cb], deps=[dw], tag=f"Bc{qr}_{m}")
            pb = psB[2 + (m % 2)]
            def mmq(e, m=m):
                r = None
                for c in range(32):
                    r = e.matmul(ps[2 + (m % 2)][:, :],
                                 wbf[m % 2][:, c * 128:(c + 1) * 128],
                                 lnfm[:, c * 512:(c + 1) * 512],
                                 start=(c == 0), stop=(c == 31))
                return r
            g.op("T", mmq, reads=[cb, buf("lnfm")], writes=[pb], tag=f"Bm{qr}_{m}")
            g.op("A", lambda e, m=m, qr=qr: e.activation(
                qlat[m][:, qr * 512:(qr + 1) * 512], ps[2 + (m % 2)][:, :], AF.Identity,
                bias=bqc[:, m:m + 1], scale=constv[:, 5:6]),
                reads=[pb, buf("bqc_t"), buf("cvec")], writes=[buf("qlat")],
                tag=f"Be{qr}_{m}")

    if tap == "lat":
        fin = []
        for m in range(4):
            fin.append(dma("P", t_kv[m * 128:(m + 1) * 128, :], kvlat[m][:, :],
                           reads=[buf("kvlat")], tag=f"tkv{m}"))
        fin.append(dma("P", t_kv[4 * 128:4 * 128 + 64, :], kvrope[0:64, :],
                       reads=[buf("kvlat")], tag="tkvr"))
        for m in range(12):
            fin.append(dma("P", t_q[m * 128:(m + 1) * 128, :], qlat[m][:, :],
                           reads=[buf("qlat")], tag=f"tq{m}"))
        g.op("P", lambda e: e.nop(), deps=fin, tag="fin")
        return nc, g

    g.barrier()

    g.op("P", lambda e: e.memset(osum[:, :], 0.0), writes=[buf("osum")], tag="os0")
    for par in range(2):
        g.op("P", lambda e, par=par: e.memset(
            Vt[par][:, :].rearrange("p (k w) -> p k w", k=32)[:, :, 128:129], 1.0),
            writes=[buf(f"Vt{par}")], tag=f"vone{par}")

    def gen_pair(pr):
        par = pr % 2
        dw = dma("S", quw32[:, :], dr["Wqup_t"][pr], writes=[buf("quw32")],
                 tag=f"qw{pr}")
        g.op("V", lambda e: e.tensor_copy(quwb[:, :], quw32[:, :]),
             reads=[buf("quw32")], writes=[buf("quwb")], deps=[dw], tag=f"qwc{pr}")
        for m3 in range(3):
            for qc in range(4):
                pb = psB[4 + ((m3 * 4 + qc) % 2)]
                def mmq(e, m3=m3, qc=qc):
                    r = None
                    for c in range(12):
                        r = e.matmul(ps[4 + ((m3 * 4 + qc) % 2)][:, :],
                                     quwb[:, c * 384 + m3 * 128:c * 384 + (m3 + 1) * 128],
                                     qlat[c][:, qc * 512:(qc + 1) * 512],
                                     start=(c == 0), stop=(c == 11))
                    return r
                g.op("T", mmq, reads=[buf("quwb"), buf("qlat")], writes=[pb],
                     tag=f"qg{pr}_{m3}_{qc}")
                g.op("A", lambda e, m3=m3, qc=qc, pr=pr, par=par: e.activation(
                    qt[par][m3][:, qc * 512:(qc + 1) * 512],
                    ps[4 + ((m3 * 4 + qc) % 2)][:, :], AF.Identity,
                    bias=bqup[:, pr * 3 + m3:pr * 3 + m3 + 1], scale=constv[:, 5:6]),
                    reads=[pb, buf("bqup_t"), buf("cvec")], writes=[buf(f"qt{par}")],
                    tag=f"qe{pr}_{m3}_{qc}")

    def gen_head(h):
        par = h % 2
        dwk = dma("S", kvw32[par][:, 0:512], dr["Wkup_t"][h],
                  writes=[buf(f"kvw32_{par}")], tag=f"kw{h}")
        dwv = dma("S", kvw32[par][:, 512:1024], dr["Wvup_t"][h],
                  writes=[buf(f"kvw32_{par}")], tag=f"vw{h}")
        g.op("V", lambda e, par=par: e.tensor_copy(kvwb[par][:, :], kvw32[par][:, :]),
             reads=[buf(f"kvw32_{par}")], writes=[buf(f"kvwb{par}")],
             deps=[dwk, dwv], tag=f"kvc{h}")
        for cc in range(8):
            pb = psB[4 + (cc % 2)]
            def mmk(e, cc=cc, par=par):
                r = None
                for c in range(4):
                    r = e.matmul(ps[4 + (cc % 2)][:, :],
                                 kvwb[par][:, c * 128:(c + 1) * 128],
                                 kvlat[c][:, cc * 512:(cc + 1) * 512],
                                 start=(c == 0), stop=(c == 3))
                return r
            g.op("T", mmk, reads=[buf(f"kvwb{par}"), buf("kvlat")], writes=[pb],
                 tag=f"kg{h}_{cc}")
            g.op("A", lambda e, cc=cc, par=par, h=h: e.activation(
                KT[par][:, cc * 512:(cc + 1) * 512], ps[4 + (cc % 2)][:, :], AF.Identity,
                bias=bkup[:, h:h + 1], scale=constv[:, 5:6]),
                reads=[pb, buf("bkup_t"), buf("cvec")], writes=[buf(f"KT{par}")],
                tag=f"ke{h}_{cc}")
        for kq in range(8):
            pb = psB[4 + (kq % 2)]
            def mmv(e, kq=kq, par=par):
                r = None
                for j in range(4):
                    ks = kq * 4 + j
                    for c in range(4):
                        r = e.matmul(ps[4 + (kq % 2)][:, j * 128:(j + 1) * 128],
                                     kvlat[c][:, ks * 128:(ks + 1) * 128],
                                     kvwb[par][:, 512 + c * 128:512 + (c + 1) * 128],
                                     start=(c == 0), stop=(c == 3))
                return r
            g.op("T", mmv, reads=[buf(f"kvwb{par}"), buf("kvlat")], writes=[pb],
                 tag=f"vg{h}_{kq}")
            def evv(e, kq=kq, par=par):
                r = None
                for j in range(4):
                    ks = kq * 4 + j
                    r = e.tensor_copy(Vt[par][:, ks * VSTRIDE:ks * VSTRIDE + 128],
                                      ps[4 + (kq % 2)][:, j * 128:(j + 1) * 128])
                return r
            g.op("V", evv, reads=[pb], writes=[buf(f"Vt{par}")], tag=f"ve{h}_{kq}")

    def attend(h):
        par = h % 2
        hp = h % 2
        qpar = (h // 2) % 2
        nope = qt[qpar][hp]
        ropeq = qt[qpar][2]
        for qc in range(4):
            sc_ops = []
            for kt in range(32):
                sbB = psB[kt % 2]
                rope_t = kvrope if hp == 0 else kvropeB
                def mms(e, kt=kt, qc=qc, rope_t=rope_t):
                    e.matmul(ps[kt % 2][:, :], KT[par][:, kt * 128:(kt + 1) * 128],
                             nope[:, qc * 512:(qc + 1) * 512], start=True, stop=False)
                    return e.matmul(ps[kt % 2][:, :],
                                    rope_t[:, kt * 128:(kt + 1) * 128],
                                    ropeq[:, qc * 512:(qc + 1) * 512],
                                    start=False, stop=True)
                g.op("T", mms, reads=[buf(f"KT{par}"), buf("kvlat"), buf("kvropeB"),
                                      buf(f"qt{qpar}")], writes=[sbB],
                     tag=f"s{h}_{qc}_{kt}")
                g.op("A", lambda e, kt=kt: e.activation(pt[kt % 2][:, :],
                                                        ps[kt % 2][:, :],
                                                        AF.Exp,
                                                        bias=constv[:, 4:5],
                                                        scale=constv[:, 2:3]),
                     reads=[sbB, buf("cvec")], writes=[buf(f"pt{kt % 2}")],
                     tag=f"x{h}_{qc}_{kt}")
                if kt > 0:
                    emit_pv(h, qc, kt - 1, par)
            emit_pv(h, qc, 31, par)
            for qs in range(4):
                ob = psB[2 + qs]
                g.op("V", lambda e, qs=qs: e.reciprocal(rcp[:, qs:qs + 1],
                                                        ps[2 + qs][:, 128:129]),
                     reads=[ob], writes=[buf("rcp")], tag=f"r{h}_{qc}_{qs}")
                col = (qc * 4 + qs) * 128
                g.op("V", lambda e, qs=qs, col=col: e.scalar_tensor_tensor(
                    osum[:, col:col + 128], ps[2 + qs][:, 0:128], rcp[:, qs:qs + 1],
                    osum[:, col:col + 128], ALU.mult, ALU.add),
                    reads=[ob, buf("rcp"), buf("osum")], writes=[buf("osum")],
                    tag=f"o{h}_{qc}_{qs}")

    def emit_pv(h, qc, kt, par):
        for qs in range(4):
            ob = psB[2 + qs]
            g.op("T", lambda e, kt=kt, qs=qs: e.matmul(
                ps[2 + qs][:, 0:129], pt[kt % 2][:, qs * 128:(qs + 1) * 128],
                Vt[par][:, kt * VSTRIDE:kt * VSTRIDE + 129],
                start=(kt == 0), stop=(kt == 31)),
                reads=[buf(f"pt{kt % 2}"), buf(f"Vt{par}")], writes=[ob],
                tag=f"p{h}_{qc}_{kt}_{qs}")

    gen_pair(0)
    gen_head(0)
    for h in range(NH):
        if h % 2 == 1 and h + 1 < NH:
            gen_pair((h + 1) // 2)
        if h + 1 < NH:
            gen_head(h + 1)
        attend(h)

    if tap == "osum":
        f = dma("P", t_os, osum[:, :], reads=[buf("osum")], tag="tos")
        g.op("P", lambda e: e.nop(), deps=[f], tag="fin")
        return nc, g

    g.barrier()

    g.op("V", lambda e: e.tensor_copy(osT[:, :], osum[:, :]),
         reads=[buf("osum")], writes=[buf("osT")], tag="osc")
    for s4 in range(2):
        pb = psbB[s4]
        def trs(e, s4=s4):
            r = None
            for k in range(8):
                sidx = s4 * 8 + k
                r = e.transpose(psb[s4][:, k * 128:(k + 1) * 128],
                                osT[:, sidx * 128:(sidx + 1) * 128], ident[:, :])
            return r
        g.op("T", trs, reads=[buf("osT"), buf("ident")], writes=[pb], tag=f"ot{s4}")
    for s4 in range(2):
        pb = psbB[s4]
        def eve(e, s4=s4):
            r = None
            for k in range(8):
                sidx = s4 * 8 + k
                r = e.tensor_copy(osT[:, sidx * 128:(sidx + 1) * 128],
                                  psb[s4][:, k * 128:(k + 1) * 128])
            return r
        g.op("V", eve, reads=[pb], writes=[buf("osT")], tag=f"oe{s4}")

    dwo = dma("S", wostg[:, :], dr["WoT"],
              writes=[buf("xstrip"), buf("ostrip"), buf("osum")], tag="wol")
    g.op("V", lambda e: e.tensor_copy(WoTb[:, :], wostg[:, :]),
         reads=[buf("xstrip"), buf("ostrip")], writes=[buf("WoTb")], deps=[dwo],
         tag="woc")
    g.op("V", lambda e: e.tensor_reduce(Bv[:, :], bvupT[:, :], AX.X, ALU.add),
         reads=[buf("bvupT")], writes=[buf("Bv")], tag="bv")
    g.op("V", lambda e: e.tensor_scalar(osT[:, :], osT[:, :], Bv[:, 0:1], None,
                                        ALU.add),
         reads=[buf("Bv"), buf("osT")], writes=[buf("osT")], tag="bvadd")
    def bcast_tile(dst, dstb, src_dr, tag):
        for hh in range(2):
            dl = dma("S", xstrip[:, :], src_dr[:, hh * 2048:(hh + 1) * 2048],
                     writes=[buf("xstrip")], tag=f"{tag}l{hh}")
            g.op("V", lambda e, hh=hh: e.tensor_copy(
                dst[:, hh * 2048:(hh + 1) * 2048], xstrip[:, :]),
                reads=[buf("xstrip")], writes=[dstb], deps=[dl], tag=f"{tag}c{hh}")

    bcast_tile(bobc, buf("bobc"), dr["bo_tile"], "bo")
    bcast_tile(fbbc, buf("fbbc"), dr["fb_tile"], "fb")

    def p3_half(half):
        for s in range(8):
            sidx = half * 8 + s
            tok0 = sidx * 128
            xb = buf("xt3")
            dx = dma("P", xt3[:, :], xkv[tok0:tok0 + 128, :], writes=[xb],
                     tag=f"Cx{sidx}")
            g.op("V", lambda e: e.tensor_tensor(xt3[:, :], xt3[:, :], bobc[:, :],
                                                ALU.add),
                 reads=[xb, buf("bobc")], writes=[xb], deps=[dx], tag=f"Cb{sidx}")
            for occ in range(8):
                pb = psB[occ % 4]
                def mmo(e, occ=occ, sidx=sidx):
                    return e.matmul(ps[occ % 4][:, :],
                                    osT[:, sidx * 128:(sidx + 1) * 128],
                                    WoTb[:, occ * 512:(occ + 1) * 512],
                                    start=True, stop=True)
                g.op("T", mmo, reads=[buf("osT"), buf("WoTb")], writes=[pb],
                     tag=f"w{sidx}_{occ}")
                g.op("A", lambda e, occ=occ: e.activation(
                    xn[:, occ * 512:(occ + 1) * 512], ps[occ % 4][:, :], AF.Identity,
                    bias=constv[:, 4:5], scale=constv[:, 5:6]),
                    reads=[pb, buf("cvec")], writes=[buf("xn")], tag=f"wa{sidx}_{occ}")
                g.op("V", lambda e, occ=occ: e.tensor_tensor(
                    xn[:, occ * 512:(occ + 1) * 512], xn[:, occ * 512:(occ + 1) * 512],
                    xt3[:, occ * 512:(occ + 1) * 512], ALU.add),
                    reads=[xb, buf("xn")], writes=[buf("xn")], deps=[dx],
                    tag=f"we{sidx}_{occ}")
            xnb = buf("xn")
            dxn = dma("P", xnew_hbm[tok0:tok0 + 128, :], xn[:, :], reads=[xnb],
                      tag=f"xw{sidx}")
            if tap == "xnew":
                dma("P", t_xn[tok0:tok0 + 128, :], xn[:, :], reads=[xnb],
                    tag=f"txn{sidx}")
            ln_tile(xn[:, :], xnb, lntm2, buf("lntm2"), f"C{sidx}")
            transposes(lntm2, buf("lntm2"), s, lnfm2, buf("lnfm2"), 1024,
                       f"Ct{sidx}")
        return

    NPAN = 16
    out_dmas = []

    def p4_half(half):
        for piece in range(4):
            dwa = dma("S", xstrip[:, :].rearrange("p (a w) -> p a w", a=8),
                      dr["A_t"][piece * 8:(piece + 1) * 8].rearrange(
                          "a p m w -> p a (m w)"),
                      writes=[buf("xstrip")], tag=f"al{half}_{piece}")
            g.op("V", lambda e, piece=piece: e.tensor_copy(
                ATb[:, piece * 2048:(piece + 1) * 2048], xstrip[:, :]),
                reads=[buf("xstrip")], writes=[ZW01], deps=[dwa],
                tag=f"ac{half}_{piece}")
        for m in range(2):
            for qc2 in range(2):
                pb = psB[m * 2 + qc2]
                def mma(e, m=m, qc2=qc2):
                    r = None
                    for c in range(32):
                        r = e.matmul(
                            ps[m * 2 + qc2][:, :],
                            ATb[:, c * 256 + m * 128:c * 256 + (m + 1) * 128],
                            lnfm2[:, c * 1024 + qc2 * 512:c * 1024 + (qc2 + 1) * 512],
                            start=(c == 0), stop=(c == 31))
                    return r
                g.op("T", mma, reads=[ZW01, buf("lnfm2")], writes=[pb],
                     tag=f"xa{half}_{m}_{qc2}")
                g.op("V", lambda e, m=m, qc2=qc2: e.tensor_scalar(
                    xa[:, m * 1024 + qc2 * 512:m * 1024 + (qc2 + 1) * 512],
                    ps[m * 2 + qc2][:, :], wvec[:, m:m + 1], None, ALU.mult),
                    reads=[pb, buf("wvec")], writes=[buf("xa")],
                    tag=f"xe{half}_{m}_{qc2}")
        if half == 0:
            for c2 in range(2):
                for hh in range(2):
                    dwb = dma("S", xstrip[:, :],
                              dr["Bf_t"][c2][:, hh * 2048:(hh + 1) * 2048],
                              writes=[buf("xstrip")], tag=f"bl{c2}_{hh}")
                    g.op("V", lambda e, c2=c2, hh=hh: e.tensor_copy(
                        BfTb[:, c2 * 4096 + hh * 2048:c2 * 4096 + (hh + 1) * 2048],
                        xstrip[:, :]),
                        reads=[buf("xstrip")], writes=[buf("BfTb")], deps=[dwb],
                        tag=f"bc{c2}_{hh}")
        for p in range(NPAN):
            xsb = buf("xstrip")
            for qd in range(4):
                dwc = dma("S", xstrip[:, :], dr["WffnT_t"][p][qd],
                          writes=[xsb], tag=f"fl{half}_{p}_{qd}")
                g.op("V", lambda e, qd=qd, p=p: e.tensor_copy(
                    wpan[p % 2][:, qd * 2048:(qd + 1) * 2048], xstrip[:, :]),
                    reads=[xsb], writes=[ZW01], deps=[dwc],
                    tag=f"fc{half}_{p}_{qd}")
            dxs = dma("P", xstrip[:, :].rearrange("q (s w) -> q s w", s=8),
                      xnew_hbm[half * 1024:(half + 1) * 1024,
                               p * 256:(p + 1) * 256].rearrange(
                          "(s q) w -> q s w", q=128),
                      writes=[xsb], tag=f"xs{half}_{p}")
            g.op("V", lambda e, p=p: e.tensor_tensor(
                xstrip[:, :].rearrange("q (s w) -> q s w", s=8),
                xstrip[:, :].rearrange("q (s w) -> q s w", s=8),
                bass.AP(fbbc, p * 256, [[4096, 128], [0, 8], [1, 256]]),
                ALU.add),
                reads=[xsb, buf("fbbc")], writes=[xsb], deps=[dxs],
                tag=f"xb{half}_{p}")
            for s in range(8):
                pb = psB[4 + (s % 2)]
                def mmw(e, p=p, s=s):
                    for c in range(32):
                        e.matmul(ps[4 + (s % 2)][:, 0:256],
                                 lnfm2[:, c * 1024 + s * 128:c * 1024 + (s + 1) * 128],
                                 wpan[p % 2][:, c * 256:(c + 1) * 256],
                                 start=(c == 0), stop=False)
                    r = None
                    for c2 in range(2):
                        r = e.matmul(ps[4 + (s % 2)][:, 0:256],
                                     xa[:, c2 * 1024 + s * 128:c2 * 1024 + (s + 1) * 128],
                                     BfTb[:, c2 * 4096 + p * 256:c2 * 4096 + (p + 1) * 256],
                                     start=False, stop=(c2 == 1))
                    return r
                g.op("T", mmw, reads=[buf("lnfm2"), ZW01, buf("xa"),
                                      buf("BfTb")],
                     writes=[pb], tag=f"f{half}_{p}_{s}")
                g.op("A", lambda e, p=p, s=s: e.activation(
                    ostrip[:, s * 256:(s + 1) * 256], ps[4 + (s % 2)][:, 0:256],
                    AF.Identity, bias=constv[:, 4:5], scale=constv[:, 5:6]),
                    reads=[pb, buf("cvec")], writes=[buf("ostrip")],
                    tag=f"fa{half}_{p}_{s}")
                g.op("V", lambda e, p=p, s=s: e.tensor_tensor(
                    ostrip[:, s * 256:(s + 1) * 256], ostrip[:, s * 256:(s + 1) * 256],
                    xstrip[:, s * 256:(s + 1) * 256], ALU.add),
                    reads=[xsb, buf("ostrip")], writes=[buf("ostrip")], deps=[dxs],
                    tag=f"fe{half}_{p}_{s}")
            do = dma("P", out[half * 1024:(half + 1) * 1024,
                              p * 256:(p + 1) * 256].rearrange(
                "(s q) w -> q s w", q=128),
                ostrip[:, :].rearrange("q (s w) -> q s w", s=8),
                reads=[buf("ostrip")], tag=f"ow{half}_{p}")
            out_dmas.append(do)

    for half in range(2):
        p3_half(half)
        if tap == "xnew":
            continue
        p4_half(half)

    if tap == "xnew":
        g.op("P", lambda e: e.nop(), deps=[g.last["dP"]], tag="fin")
        return nc, g

    g.op("P", lambda e: e.nop(), deps=out_dmas, tag="fin")
    return nc, g


def finish(nc, g):
    with (
        nc.semaphore("semT") as sT, nc.semaphore("semV") as sV,
        nc.semaphore("semA") as sA, nc.semaphore("semP") as sP,
        nc.semaphore("semS") as sS, nc.semaphore("semdS") as sdS,
        nc.semaphore("semdP") as sdP,
        nc.Block() as block,
    ):
        sems = {"T": sT, "V": sV, "A": sA, "P": sP, "S": sS, "dS": sdS, "dP": sdP}
        run = g.emit_streams(sems)

        @block.tensor
        def _(e):
            run("T", e)

        @block.vector
        def _(e):
            run("V", e)

        @block.scalar
        def _(e):
            run("A", e)

        @block.gpsimd
        def _(e):
            run("P", e)

        @block.sync
        def _(e):
            run("S", e)
    return nc


_PROG = {}


def get_program(tap=None):
    if tap not in _PROG:
        nc, g = build_program(tap)
        _PROG[tap] = finish(nc, g)
    return _PROG[tap]


def make_in_maps(inputs):
    x = np.asarray(inputs["L0_attn_hidden_size"])
    w = host_layout(inputs)
    in_maps = []
    for c in range(8):
        b, half = c // 2, c % 2
        xs = np.ascontiguousarray(np.roll(x[b], -half * OWN, axis=0))
        m = {"xkv": xs}
        m.update(w)
        in_maps.append(m)
    return in_maps


def kernel(**inputs):
    nc = get_program(None)
    in_maps = make_in_maps(inputs)
    res = run_bass_kernel_spmd(nc, in_maps, core_ids=list(range(8)))
    outv = np.empty((4, SEQ, DM), dtype=np.float32)
    for c in range(8):
        b, half = c // 2, c % 2
        outv[b, half * OWN:(half + 1) * OWN] = res.results[c]["out"]
    return outv
